# version 04
# baseline (speedup 1.0000x reference)
import contextlib
import numpy as np
import concourse.bass as bass
import concourse.mybir as mybir
from concourse.bass_utils import run_bass_kernel_spmd

F32 = mybir.dt.float32
BF16 = mybir.dt.bfloat16
I8 = mybir.dt.int8
I32 = mybir.dt.int32
AF = mybir.ActivationFunctionType
ALU = mybir.AluOpType
AX = mybir.AxisListType.X

NCORE = 8
D = 1024
T = 2048
HALO = 128
TT = T + HALO
NTILE = TT // 128
NOWN = T // 128
DH = 64
GH = 12
GD = GH * DH
NCHUNK = T // 64
SW = 390
NEGC = -12.0
C23 = 12582912.0
NPOOL = 4


NWAVE = 28


def _l1_sched():
    uses = []
    for w in range(NWAVE):
        if 0 <= w - 6 < NTILE:
            uses.append(("t34", w - 6))
            uses.append(("t34x", w - 6))
        if 0 <= w - 5 < NTILE:
            uses.append(("t12", w - 5))
            uses.append(("t12x", w - 5))
        if 0 <= w - 10 < NOWN:
            uses.append(("cx", 2 * (w - 10)))
            uses.append(("cx", 2 * (w - 10) + 1))
    evn = 0
    waits, banks, evcnt = [], [], {}
    last = {}
    for n, use in enumerate(uses):
        b = n % NPOOL
        if use[0] != "cx":
            evn += 1
            evcnt[use] = evn
        if b in last:
            kind = last[b]
            if kind[0] == "cx":
                waits.append(("su", kind[1] + 1))
            else:
                waits.append(("ev", evcnt[kind]))
        else:
            waits.append(None)
        banks.append(b)
        last[b] = use
    return uses, banks, waits, evcnt



def build_l1(detect=False, stage=9):
    nc = bass.Bass("TRN2", target_bir_lowering=False, debug=False,
                   detect_race_conditions=detect)

    xsl = nc.dram_tensor("xsl", [TT, D], F32, kind="ExternalInput").ap()
    wkg = nc.dram_tensor("wkg", [D, GD], F32, kind="ExternalInput").ap()
    wvg = nc.dram_tensor("wvg", [D, GD], F32, kind="ExternalInput").ap()
    csts = nc.dram_tensor("csts", [128, 6], F32, kind="ExternalInput").ap()
    idn = nc.dram_tensor("idn", [128, 128], BF16, kind="ExternalInput").ap()

    xnfm = nc.dram_tensor("xnfm", [D, TT], BF16, kind="ExternalOutput").ap()
    xqfm = nc.dram_tensor("xqfm", [D, TT], BF16, kind="ExternalOutput").ap()
    gam = nc.dram_tensor("gam", [128, NOWN], F32, kind="ExternalOutput").ap()
    states = nc.dram_tensor("states", [NCHUNK + 1, 128, SW], F32,
                            kind="ExternalOutput").ap()
    dbg = nc.dram_tensor("dbg", [128, 8 * NTILE], F32, kind="ExternalOutput").ap()

    uses, banks, uwaits, evcnt = _l1_sched()
    ub, uw = {}, {}
    for n, use in enumerate(uses):
        ub[use] = banks[n]
        uw[use] = uwaits[n]
    NGRP = 5
    gtiles = [list(range(4 * g, min(4 * g + 4, NTILE))) for g in range(NGRP)]
    mathwave = [4 * g + 3 for g in range(4)] + [17]

    ctx = contextlib.ExitStack()
    with ctx:
        sb = lambda name, shape, dt: ctx.enter_context(nc.sbuf_tensor(name, shape, dt))
        sem = lambda name: ctx.enter_context(nc.semaphore(name))

        xb1 = [sb(f"xb1{i}", [128, D], F32) for i in range(3)]
        xb2 = [sb(f"xb2{i}", [128, D], F32) for i in range(2)]
        ct = sb("ct", [128, 6], F32)
        idt = sb("idt", [128, 128], BF16)
        sqd = sb("sqd", [128, D], F32)
        asq = sb("asq", [128, D], F32)
        sum_a = sb("sum_a", [128, NTILE], F32)
        ssq_a = sb("ssq_a", [128, NTILE], F32)
        mean_a = sb("mean_a", [128, NTILE], F32)
        var_a = sb("var_a", [128, NTILE], F32)
        sqa_a = sb("sqa_a", [128, NTILE], F32)
        rstd_a = sb("rstd_a", [128, NTILE], F32)
        negmr_a = sb("negmr_a", [128, NTILE], F32)
        xmax_a = sb("xmax_a", [128, NTILE], F32)
        xmin_a = sb("xmin_a", [128, NTILE], F32)
        tmax_a = sb("tmax_a", [128, NTILE], F32)
        tmin_a = sb("tmin_a", [128, NTILE], F32)
        tmpb_a = sb("tmpb_a", [128, NTILE], F32)
        amax_a = sb("amax_a", [128, NTILE], F32)
        tmp2_a = sb("tmp2_a", [128, NTILE], F32)
        rr_a = sb("rr_a", [128, NTILE], F32)
        asc_a = sb("asc_a", [128, NTILE], F32)
        gam_a = sb("gam_a", [128, NTILE], F32)

        sca2_a = sb("sca2_a", [128, NTILE], F32)
        nb2c_a = sb("nb2c_a", [128, NTILE], F32)
        xq_f = [sb(f"xq_f{i}", [128, D], F32) for i in range(2)]
        xn_b = [sb(f"xn_b{i}", [128, D], BF16) for i in range(2)]
        xq_i = [sb(f"xq_i{i}", [128, D], I8) for i in range(2)]
        xq_b = [sb(f"xq_b{i}", [128, D], BF16) for i in range(2)]
        xn_st = [sb(f"xn_st{i}", [128, D], BF16) for i in range(2)]
        xq_st = [sb(f"xq_st{i}", [128, D], BF16) for i in range(2)]

        wk_f = [sb(f"wk_f{i}", [128, GD], F32) for i in range(2)]
        wv_f = [sb(f"wv_f{i}", [128, GD], F32) for i in range(2)]
        wkb = [sb(f"wkb{c}", [128, GD], BF16) for c in range(8)]
        wvb = [sb(f"wvb{c}", [128, GD], BF16) for c in range(8)]

        etmp = sb("etmp", [128, GD], F32)
        kmx = [sb(f"kmx{i}", [128, GD], F32) for i in range(2)]
        eexp = [sb(f"eexp{i}", [128, GD], F32) for i in range(2)]
        ktm = [sb(f"ktm{j}", [128, GD], BF16) for j in range(NOWN)]
        vtm = [sb(f"vtm{j}", [128, 780], BF16) for j in range(NOWN)]
        st_ring = sb("st_ring", [128, 3 * SW], F32)

        pool = [ctx.enter_context(nc.psum_tensor(f"pp{i}", [128, 512], F32))
                for i in range(NPOOL)]
        kvt = ctx.enter_context(nc.psum_tensor("kvt", [128, 2048], F32))
        k_all = kvt[:, 0:GD]
        v_all = kvt[:, 1024:1024 + GD]

        s_c = sem("s_c")
        s_x1 = [sem(f"s_x1_{i}") for i in range(3)]
        s_x2 = [sem(f"s_x2_{i}") for i in range(2)]
        s_st1 = sem("s_st1")
        s_sa1 = sem("s_sa1")
        s_ini = sem("s_ini")
        s_vvg = sem("s_vvg")
        s_sqg = sem("s_sqg")
        s_smg = sem("s_smg")
        s_wlk = [sem(f"s_wlk{i}") for i in range(2)]
        s_wlv = [sem(f"s_wlv{i}") for i in range(2)]
        s_wc = sem("s_wc")
        s_xnb = sem("s_xnb")
        s_xqf = sem("s_xqf")
        s_xqi = sem("s_xqi")
        s_qb = sem("s_qb")
        s_t12 = sem("s_t12")
        s_t34 = sem("s_t34")
        s_ev = sem("s_ev")
        s_soxn = [sem(f"s_soxn{i}") for i in range(2)]
        s_soxq = [sem(f"s_soxq{i}") for i in range(2)]
        s_mm = sem("s_mm")
        s_et = sem("s_et")
        s_km = sem("s_km")
        s_ee = sem("s_ee")
        s_kv = sem("s_kv")
        s_vv = sem("s_vv")
        s_cx = sem("s_cx")
        s_su = sem("s_su")
        s_sd = [sem(f"s_sd{i}") for i in range(3)]
        s_fo = sem("s_fo")
        s_vr = sem("s_vr")

        blk = ctx.enter_context(nc.Block())

        def so_cnt(i):
            return 16 * ((i - (i % 2)) // 2 + 1)

        @blk.sync
        def _(sync):
            sync.dma_start(out=ct[:], in_=csts).then_inc(s_c, 16)
            sync.dma_start(out=idt[:], in_=idn).then_inc(s_c, 16)
            for w in range(NWAVE):
                if w < NTILE:
                    i = w
                    if i >= 3:
                        sync.wait_ge(s_st1, i - 2)
                        sync.wait_ge(s_sa1, i - 2)
                    sync.dma_start(out=xb1[i % 3][:],
                                   in_=xsl[128 * i:128 * (i + 1), :]
                                   ).then_inc(s_x1[i % 3], 16)
                if w < 8:
                    c = w
                    if c >= 2:
                        sync.wait_ge(s_wc, 2 * c - 3)
                    sync.dma_start(out=wk_f[c % 2][:],
                                   in_=wkg[128 * c:128 * (c + 1), :]
                                   ).then_inc(s_wlk[c % 2], 16)
                    if c >= 2:
                        sync.wait_ge(s_wc, 2 * c - 2)
                    sync.dma_start(out=wv_f[c % 2][:],
                                   in_=wvg[128 * c:128 * (c + 1), :]
                                   ).then_inc(s_wlv[c % 2], 16)
                i2 = w - 2
                if 0 <= i2 < NTILE:
                    if i2 >= 2:
                        sync.wait_ge(s_xnb, i2 - 1)
                    sync.dma_start(out=xb2[i2 % 2][:],
                                   in_=xsl[128 * i2:128 * (i2 + 1), :]
                                   ).then_inc(s_x2[i2 % 2], 16)

        @blk.vector
        def _(vector):
            vrn = [0]

            def vrt(instr):
                vrn[0] += 1
                instr.then_inc(s_vr, 1)
                vector.wait_ge(s_vr, vrn[0])

            for j in range(NOWN):
                vector.memset(
                    vtm[j][:].rearrange("p (h c) -> p h c", h=GH)[:, :, 64:65], 1.0)
            vector.memset(st_ring[:, 0:SW], 0.0).then_inc(s_ini, 1)

            def v_math(g):
                c0, c1 = 4 * g, min(4 * g + 4, NTILE)
                sl = slice(c0, c1)
                vector.wait_ge(s_sa1, c1)
                vrt(vector.tensor_scalar(out=mean_a[:, sl], in0=sum_a[:, sl],
                                         scalar1=1.0 / D, scalar2=None,
                                         op0=ALU.mult))
                vector.tensor_tensor(out=tmpb_a[:, sl], in0=mean_a[:, sl],
                                     in1=mean_a[:, sl], op=ALU.mult)
                vector.tensor_tensor(out=tmax_a[:, sl], in0=xmax_a[:, sl],
                                     in1=mean_a[:, sl], op=ALU.subtract)
                vrt(vector.tensor_tensor(out=tmin_a[:, sl], in0=mean_a[:, sl],
                                         in1=xmin_a[:, sl], op=ALU.subtract))
                vector.scalar_tensor_tensor(out=var_a[:, sl], in0=ssq_a[:, sl],
                                            scalar=1.0 / D, in1=tmpb_a[:, sl],
                                            op0=ALU.mult, op1=ALU.subtract)
                vector.tensor_tensor(out=amax_a[:, sl], in0=tmax_a[:, sl],
                                     in1=tmin_a[:, sl],
                                     op=ALU.max).then_inc(s_vvg, 1)
                vector.wait_ge(s_sqg, g + 1)
                vrt(vector.reciprocal(out=rstd_a[:, sl], in_=sqa_a[:, sl]))
                vector.scalar_tensor_tensor(out=negmr_a[:, sl],
                                            in0=mean_a[:, sl], scalar=-1.0,
                                            in1=rstd_a[:, sl], op0=ALU.mult,
                                            op1=ALU.mult)
                vrt(vector.tensor_tensor(out=amax_a[:, sl], in0=amax_a[:, sl],
                                         in1=rstd_a[:, sl], op=ALU.mult))
                vrt(vector.tensor_scalar(out=tmp2_a[:, sl], in0=amax_a[:, sl],
                                         scalar1=1e-5, scalar2=None,
                                         op0=ALU.max))
                vrt(vector.reciprocal(out=rr_a[:, sl], in_=tmp2_a[:, sl]))
                vrt(vector.tensor_scalar(out=asc_a[:, sl], in0=rr_a[:, sl],
                                         scalar1=127.0, scalar2=None,
                                         op0=ALU.mult))
                vector.tensor_tensor(out=sca2_a[:, sl], in0=asc_a[:, sl],
                                     in1=rstd_a[:, sl], op=ALU.mult)
                vector.tensor_tensor(out=nb2c_a[:, sl], in0=asc_a[:, sl],
                                     in1=negmr_a[:, sl], op=ALU.mult)
                vector.tensor_scalar(out=gam_a[:, sl], in0=tmp2_a[:, sl],
                                     scalar1=1.0 / 127.0, scalar2=ct[:, 1:2],
                                     op0=ALU.mult, op1=ALU.mult).then_inc(s_smg, 1)

            for w in range(NWAVE):
                i6v = w - 6
                if 0 <= i6v < NTILE:
                    vector.wait_ge(s_xqi, i6v + 1)
                    if i6v >= 2:
                        vector.wait_ge(s_t34, 2 * (i6v - 2) + 2)
                    vector.tensor_copy(out=xq_b[i6v % 2][:],
                                       in_=xq_i[i6v % 2][:]).then_inc(s_qb, 1)
                j3 = w - 11
                if 0 <= j3 < NOWN:
                    for s in range(2):
                        u = 2 * j3 + s
                        vector.wait_ge(s_cx, u + 1)
                        if u >= 2:
                            vector.wait_ge(s_sd[(u + 1) % 3],
                                           16 * ((u - 2) // 3 + 1))
                        vector.tensor_tensor(
                            out=st_ring[:, SW * ((u + 1) % 3):SW * ((u + 1) % 3) + SW],
                            in0=st_ring[:, SW * (u % 3):SW * (u % 3) + SW],
                            in1=pool[ub[("cx", u)]][:, 0:SW],
                            op=ALU.add).then_inc(s_su, 1)
                        vector.wait_ge(s_su, u + 1)
                j8 = w - 8
                if 0 <= j8 < NOWN:
                    vector.wait_ge(s_mm, j8 + 1)
                    if j8 >= 1:
                        vector.wait_ge(s_ee, j8)
                    vector.tensor_scalar(out=etmp[:], in0=k_all, scalar1=0.0,
                                         scalar2=None,
                                         op0=ALU.min).then_inc(s_et, 1)
                    vector.tensor_scalar(out=kmx[j8 % 2][:], in0=k_all,
                                         scalar1=0.0, scalar2=None,
                                         op0=ALU.max).then_inc(s_km, 1)
                if w < NTILE:
                    i = w
                    vector.wait_ge(s_x1[i % 3], 16 * (i // 3 + 1))
                    if i >= 1:
                        vector.wait_ge(s_st1, i)
                    x_ap = xb1[i % 3][:]
                    vector.reduce_max(out=xmax_a[:, i:i + 1], in_=x_ap, axis=AX)
                    vector.tensor_reduce(out=xmin_a[:, i:i + 1], in_=x_ap,
                                         axis=AX, op=ALU.min)
                    vector.scalar_tensor_tensor(
                        out=sqd[:], in0=x_ap, scalar=1.0, in1=x_ap,
                        op0=ALU.mult, op1=ALU.mult,
                        accum_out=ssq_a[:, i:i + 1]).then_inc(s_st1, 1)
                if w in mathwave:
                    g = mathwave.index(w)
                    vector.wait_ge(s_c, 32)
                    v_math(g)
                i5 = w - 5
                if 0 <= i5 < NTILE:
                    vector.wait_ge(s_xqf, i5 + 1)
                    if i5 >= 2:
                        vector.wait_ge(s_qb, i5 - 1)
                    vector.tensor_scalar(out=xq_i[i5 % 2][:],
                                         in0=xq_f[i5 % 2][:], scalar1=C23,
                                         scalar2=C23, op0=ALU.add,
                                         op1=ALU.subtract).then_inc(s_xqi, 1)
                j9 = w - 9
                if 0 <= j9 < NOWN:
                    vector.wait_ge(s_ee, j9 + 1)
                    vector.tensor_tensor(out=ktm[j9][:], in0=kmx[j9 % 2][:],
                                         in1=eexp[j9 % 2][:],
                                         op=ALU.add).then_inc(s_kv, 1)

        @blk.scalar
        def _(scalar):
            scalar.wait_ge(s_c, 32)
            for w in range(NWAVE):
                i6 = w - 6
                j8 = w - 8
                if 0 <= j8 < NOWN:
                    scalar.wait_ge(s_mm, j8 + 1)
                    vt3 = vtm[j8][:].rearrange("p (h c) -> p h c", h=GH)
                    scalar.copy(out=vt3[:, :, 0:64],
                                in_=v_all.rearrange("p (h c) -> p h c",
                                                    h=GH)).then_inc(s_vv, 1)
                if w < NTILE:
                    i = w
                    scalar.wait_ge(s_x1[i % 3], 16 * (i // 3 + 1))
                    if i >= 1:
                        scalar.wait_ge(s_sa1, i)
                    scalar.activation(out=asq[:], in_=xb1[i % 3][:],
                                      func=AF.Copy, scale=1.0,
                                      accum_out=sum_a[:, i:i + 1]
                                      ).then_inc(s_sa1, 1)
                if w < 8:
                    c = w
                    scalar.wait_ge(s_wlk[c % 2], 16 * (c // 2 + 1))
                    scalar.copy(out=wkb[c][:], in_=wk_f[c % 2][:]).then_inc(s_wc, 1)
                    scalar.wait_ge(s_wlv[c % 2], 16 * (c // 2 + 1))
                    scalar.copy(out=wvb[c][:], in_=wv_f[c % 2][:]).then_inc(s_wc, 1)
                i4 = w - 4
                if 0 <= i4 < NTILE:
                    g4 = i4 // 4
                    scalar.wait_ge(s_x2[i4 % 2], 16 * (i4 // 2 + 1))
                    scalar.wait_ge(s_smg, g4 + 1)
                    if i4 >= 2:
                        scalar.wait_ge(s_xqi, i4 - 1)
                    scalar.activation(out=xq_f[i4 % 2][:], in_=xb2[i4 % 2][:],
                                      func=AF.Identity,
                                      scale=sca2_a[:, i4:i4 + 1],
                                      bias=nb2c_a[:, i4:i4 + 1]
                                      ).then_inc(s_xqf, 1)
                    if i4 >= 2:
                        scalar.wait_ge(s_t12, 2 * (i4 - 2) + 2)
                    scalar.activation(out=xn_b[i4 % 2][:], in_=xb2[i4 % 2][:],
                                      func=AF.Identity,
                                      scale=rstd_a[:, i4:i4 + 1],
                                      bias=negmr_a[:, i4:i4 + 1]
                                      ).then_inc(s_xnb, 1)
                if w in mathwave:
                    g = mathwave.index(w)
                    c0, c1 = 4 * g, min(4 * g + 4, NTILE)
                    scalar.wait_ge(s_vvg, g + 1)
                    scalar.activation(out=sqa_a[:, c0:c1], in_=var_a[:, c0:c1],
                                      func=AF.Sqrt, bias=ct[:, 0:1],
                                      scale=1.0).then_inc(s_sqg, 1)
                if 0 <= i6 < NTILE:
                    for ci, key in enumerate([("t34", i6), ("t34x", i6)]):
                        scalar.wait_ge(s_t34, 2 * i6 + 1 + ci)
                        if ci == 0 and i6 >= 2:
                            scalar.wait_ge(s_soxq[i6 % 2], so_cnt(i6 - 2))
                        scalar.copy(out=xq_st[i6 % 2][:, 512 * ci:512 * (ci + 1)],
                                    in_=pool[ub[key]][:]).then_inc(s_ev, 1)
                i5 = w - 5
                if 0 <= i5 < NTILE:
                    for ci, key in enumerate([("t12", i5), ("t12x", i5)]):
                        scalar.wait_ge(s_t12, 2 * i5 + 1 + ci)
                        if ci == 0 and i5 >= 2:
                            scalar.wait_ge(s_soxn[i5 % 2], so_cnt(i5 - 2))
                            scalar.wait_ge(s_mm, i5 - 2)
                        scalar.copy(out=xn_st[i5 % 2][:, 512 * ci:512 * (ci + 1)],
                                    in_=pool[ub[key]][:]).then_inc(s_ev, 1)
                if 0 <= j8 < NOWN:
                    scalar.wait_ge(s_et, j8 + 1)
                    if j8 >= 1:
                        scalar.wait_ge(s_kv, j8)
                    scalar.activation(out=eexp[j8 % 2][:], in_=etmp[:],
                                      func=AF.Exp, bias=ct[:, 2:3],
                                      scale=1.0).then_inc(s_ee, 1)

        @blk.tensor
        def _(tensor):
            tensor.wait_ge(s_c, 32)
            tensor.wait_ge(s_ini, 1)

            def bank_wait(key):
                bw = uw[key]
                if bw is not None:
                    tensor.wait_ge(s_ev if bw[0] == "ev" else s_su, bw[1])

            for w in range(NWAVE):
                i6 = w - 6
                if 0 <= i6 < NTILE:
                    tensor.wait_ge(s_qb, i6 + 1)
                    for ci, key in enumerate([("t34", i6), ("t34x", i6)]):
                        bank_wait(key)
                        bank = pool[ub[key]]
                        for q in range(4):
                            cc = 4 * ci + q
                            mm = tensor.matmul(
                                out=bank[:, 128 * q:128 * (q + 1)],
                                lhsT=xq_b[i6 % 2][:, 128 * cc:128 * (cc + 1)],
                                rhs=idt[:], start=True, stop=True)
                        mm.then_inc(s_t34, 1)
                i5 = w - 5
                if 0 <= i5 < NTILE:
                    tensor.wait_ge(s_xnb, i5 + 1)
                    for ci, key in enumerate([("t12", i5), ("t12x", i5)]):
                        bank_wait(key)
                        bank = pool[ub[key]]
                        for q in range(4):
                            cc = 4 * ci + q
                            mm = tensor.matmul(
                                out=bank[:, 128 * q:128 * (q + 1)],
                                lhsT=xn_b[i5 % 2][:, 128 * cc:128 * (cc + 1)],
                                rhs=idt[:], start=True, stop=True)
                        mm.then_inc(s_t12, 1)
                j = w - 7
                if 0 <= j < NOWN:
                    tensor.wait_ge(s_ev, evcnt[("t12x", j + 1)])
                    if j >= 1:
                        tensor.wait_ge(s_et, j)
                        tensor.wait_ge(s_km, j)
                        tensor.wait_ge(s_vv, j)
                    stg = xn_st[(j + 1) % 2]
                    for c in range(8):
                        if j <= 1:
                            tensor.wait_ge(s_wc, 2 * c + 2)
                        st = (c == 0)
                        sp = (c == 7)
                        lhsT = stg[:, 128 * c:128 * (c + 1)]
                        tensor.matmul(out=kvt[:, 0:512], lhsT=lhsT,
                                      rhs=wkb[c][:, 0:512], start=st, stop=sp)
                        tensor.matmul(out=kvt[:, 512:768], lhsT=lhsT,
                                      rhs=wkb[c][:, 512:768], start=st, stop=sp)
                        tensor.matmul(out=kvt[:, 1024:1536], lhsT=lhsT,
                                      rhs=wvb[c][:, 0:512], start=st, stop=sp)
                        mm = tensor.matmul(out=kvt[:, 1536:1792], lhsT=lhsT,
                                           rhs=wvb[c][:, 512:768], start=st,
                                           stop=sp)
                    mm.then_inc(s_mm, 1)
                j2 = w - 10
                if 0 <= j2 < NOWN:
                    tensor.wait_ge(s_kv, j2 + 1)
                    tensor.wait_ge(s_vv, j2 + 1)
                    for s in range(2):
                        u = 2 * j2 + s
                        bank_wait(("cx", u))
                        bank = pool[ub[("cx", u)]]
                        for h in range(GH):
                            pr = h % 2
                            pj = h // 2
                            mm = tensor.matmul(
                                out=bank[64 * pr:64 * pr + 64,
                                         65 * pj:65 * pj + 65],
                                lhsT=ktm[j2][64 * s:64 * s + 64,
                                             64 * h:64 * h + 64],
                                rhs=vtm[j2][64 * s:64 * s + 64,
                                            65 * h:65 * h + 65],
                                start=True, stop=True,
                                tile_position=(64 * s, 64 * pr))
                        mm.then_inc(s_cx, 1)

        @blk.gpsimd
        def _(gpsimd):
            gpsimd.wait_ge(s_ini, 1)
            gpsimd.dma_start(out=states[0], in_=st_ring[:, 0:SW]).then_inc(s_sd[0], 16)
            for w in range(NWAVE):
                i6 = w - 6
                if 0 <= i6 < NTILE:
                    gpsimd.wait_ge(s_ev, evcnt[("t12x", i6)])
                    gpsimd.dma_start(
                        out=xnfm[:, 128 * i6:128 * (i6 + 1)].rearrange(
                            "(c p) t -> p c t", p=128),
                        in_=xn_st[i6 % 2][:].rearrange("p (c t) -> p c t", c=8)
                        ).then_inc(s_soxn[i6 % 2], 16)
                i7 = w - 7
                if 0 <= i7 < NTILE:
                    gpsimd.wait_ge(s_ev, evcnt[("t34x", i7)])
                    gpsimd.dma_start(
                        out=xqfm[:, 128 * i7:128 * (i7 + 1)].rearrange(
                            "(c p) t -> p c t", p=128),
                        in_=xq_st[i7 % 2][:].rearrange("p (c t) -> p c t", c=8)
                        ).then_inc(s_soxq[i7 % 2], 16)
                j3s = w - 12
                if 0 <= j3s < NOWN:
                    for s in range(2):
                        u = 2 * j3s + s
                        gpsimd.wait_ge(s_su, u + 1)
                        gpsimd.dma_start(
                            out=states[u + 1],
                            in_=st_ring[:, SW * ((u + 1) % 3):
                                        SW * ((u + 1) % 3) + SW]
                            ).then_inc(s_sd[(u + 1) % 3], 16)
            gpsimd.wait_ge(s_smg, NGRP)
            gpsimd.dma_start(out=gam, in_=gam_a[:, 1:NTILE]).then_inc(s_fo, 16)
            gpsimd.wait_ge(s_fo, 16)
            for di, arr in enumerate([mean_a, var_a, xmax_a, xmin_a, ssq_a, asc_a,
                                      gam_a, rstd_a]):
                gpsimd.dma_start(out=dbg[:, NTILE * di:NTILE * (di + 1)],
                                 in_=arr[:]).then_inc(s_fo, 16)
            gpsimd.wait_ge(s_fo, 16 + 8 * 16)
            for r_ in range(3):
                gpsimd.wait_ge(s_sd[r_], 16 * 11)
            gpsimd.wait_ge(s_soxn[0], 16 * 9)
            gpsimd.wait_ge(s_soxn[1], 16 * 8)
            gpsimd.wait_ge(s_soxq[0], 16 * 9)
            gpsimd.wait_ge(s_soxq[1], 16 * 8)

    return nc




LH = 4
LD = LH * DH
NG = 5
GSZ = [512, 512, 512, 512, 128]


class _Ring:

    def __init__(self, tensor, pall, sems):
        self.t = tensor
        self.pall = pall
        self.sems = sems
        self.ptr = 0
        self.last = [None] * 8
        self.uses = 0

    def claim(self, n, cons_list):
        assert len(cons_list) == n
        if self.ptr + n > 8:
            self.ptr = 0
        b0 = self.ptr
        self.ptr += n
        if self.ptr >= 8:
            self.ptr = 0
        for k, b in enumerate(range(b0, b0 + n)):
            w = self.last[b]
            if w is not None:
                self.t.wait_ge(self.sems[w[0]], w[1])
            self.last[b] = cons_list[k]
        idx = self.uses
        self.uses += 1
        return self.pall[:, 512 * b0:512 * (b0 + n)], idx


def build_l2(detect=False):
    nc = bass.Bass("TRN2", target_bir_lowering=False, debug=False,
                   detect_race_conditions=detect, dynamic_dma_scratch_size=8192)

    xown = nc.dram_tensor("xown", [T, D], F32, kind="ExternalInput").ap()
    xnfm_i = nc.dram_tensor("xnfm", [D, TT], BF16, kind="ExternalInput").ap()
    xqfm_i = nc.dram_tensor("xqfm", [D, TT], BF16, kind="ExternalInput").ap()
    gam_i = nc.dram_tensor("gam", [128, NOWN], F32, kind="ExternalInput").ap()
    sloc_i = nc.dram_tensor("sloc", [NCHUNK + 1, 128, SW], F32,
                            kind="ExternalInput").ap()
    tails_i = nc.dram_tensor("tails", [NCORE, 128, SW], F32,
                             kind="ExternalInput").ap()
    pm_i = nc.dram_tensor("pmask", [128, NCORE], F32, kind="ExternalInput").ap()
    wq_i = nc.dram_tensor("wq", [D, D], F32, kind="ExternalInput").ap()
    wo_i = nc.dram_tensor("wo", [D, D], F32, kind="ExternalInput").ap()
    wkl_i = nc.dram_tensor("wkl", [D, LD], F32, kind="ExternalInput").ap()
    wvl_i = nc.dram_tensor("wvl", [D, LD], F32, kind="ExternalInput").ap()
    bo_i = nc.dram_tensor("bo", [1, D], F32, kind="ExternalInput").ap()
    bias_i = nc.dram_tensor("bias", [1, D], F32, kind="ExternalInput").ap()
    packed_i = nc.dram_tensor("packed", [256, D], I32, kind="ExternalInput").ap()
    csts = nc.dram_tensor("csts", [128, 6], F32, kind="ExternalInput").ap()
    mloc_i = nc.dram_tensor("mloc", [128, 1024], BF16, kind="ExternalInput").ap()
    mhal_i = nc.dram_tensor("mhal", [128, 1024], BF16, kind="ExternalInput").ap()
    idn = nc.dram_tensor("idn", [128, 128], BF16, kind="ExternalInput").ap()

    out_o = nc.dram_tensor("out", [T, D], F32, kind="ExternalOutput").ap()

    ctx = contextlib.ExitStack()
    with ctx:
        sb = lambda name, shape, dt: ctx.enter_context(nc.sbuf_tensor(name, shape, dt))
        sem = lambda name: ctx.enter_context(nc.semaphore(name))

        TOP = 207872
        szA, szB, szC, szD, szE, szF = 34816, 17408, 12480, 8192, 12288, 16384
        ovlA = TOP - szA
        ovlB = ovlA - szB
        ovlC = ovlB - szC
        ovlD = ovlC - szD
        ovlE = ovlD - szE
        ovlF = ovlE - szF
        xnr_all = nc.alloc_sbuf_tensor_at("xnr_all", [128, 8 * TT], BF16,
                                          offset=ovlA)
        xnr = [xnr_all.ap()[:, TT * c:TT * (c + 1)] for c in range(8)]
        afm_t = nc.alloc_sbuf_tensor_at("afm_t", [128, 8 * T], BF16, offset=ovlA)
        afm = afm_t.ap()

        dec_all = nc.alloc_sbuf_tensor_at("dec_all", [128, 3 * D], I32,
                                          offset=ovlB)
        pkd = [dec_all.ap()[:, D * c:D * (c + 1)] for c in range(2)]
        crumb = dec_all.ap()[:, 2 * D:3 * D]
        wrow_all = nc.alloc_sbuf_tensor_at("wrow_all", [128, 2 * D], BF16,
                                           offset=ovlB + 12 * 1024)
        wrow = [wrow_all.ap()[:, D * i:D * (i + 1)] for i in range(2)]
        lfl_all = nc.alloc_sbuf_tensor_at("lfl_all", [128, 4 * TT], BF16,
                                          offset=ovlB)
        qfl = [lfl_all.ap()[:, TT * i:TT * (i + 1)] for i in range(2)]
        kfl = [lfl_all.ap()[:, TT * (2 + i):TT * (3 + i)] for i in range(2)]

        tails_t = nc.alloc_sbuf_tensor_at("tails_t", [128, NCORE * SW], F32,
                                          offset=ovlC)
        tails_sb = tails_t.ap()
        xqs_all = nc.alloc_sbuf_tensor_at("xqs_all", [128, 2 * D], BF16,
                                          offset=ovlC)
        xqs = [xqs_all.ap()[:, D * i:D * (i + 1)] for i in range(2)]
        tmpf_t = nc.alloc_sbuf_tensor_at("tmpf_t", [128, D], F32,
                                         offset=ovlC + 4 * 1024)
        tmpf = tmpf_t.ap()

        wstg_t = nc.alloc_sbuf_tensor_at("wstg_t", [128, 2 * D], F32,
                                         offset=ovlD)
        wstg = [wstg_t.ap()[:, D * i:D * (i + 1)] for i in range(2)]
        xrs_all = nc.alloc_sbuf_tensor_at("xrs_all", [128, 2 * D], F32,
                                          offset=ovlD)
        xrs = [xrs_all.ap()[:, D * i:D * (i + 1)] for i in range(2)]

        qe_all = nc.alloc_sbuf_tensor_at("qe_all", [128, 2 * GD], F32,
                                         offset=ovlE)
        qe_sb = [qe_all.ap()[:, GD * i:GD * (i + 1)] for i in range(2)]
        qtm_t = nc.alloc_sbuf_tensor_at("qtm_t", [128, 2 * GD], BF16,
                                        offset=ovlE + 6 * 1024)
        qtm = [qtm_t.ap()[:, GD * i:GD * (i + 1)] for i in range(2)]
        aw_all = nc.alloc_sbuf_tensor_at("aw_all", [128, 6 * 1024], BF16,
                                         offset=ovlE)
        attw = [aw_all.ap()[:, 1024 * i:1024 * (i + 1)] for i in range(2)]
        attn_ = [aw_all.ap()[:, 1024 * (2 + i):1024 * (3 + i)] for i in range(2)]
        attT = [aw_all.ap()[:, 1024 * (4 + i):1024 * (5 + i)] for i in range(2)]

        wqb_all = nc.alloc_sbuf_tensor_at("wqb_all", [128, 8 * D], BF16,
                                          offset=ovlF)
        wqb = [wqb_all.ap()[:, D * c:D * (c + 1)] for c in range(8)]
        wob_all = nc.alloc_sbuf_tensor_at("wob_all", [128, 8 * D], BF16,
                                          offset=ovlF)
        wob = [wob_all.ap()[:, D * c:D * (c + 1)] for c in range(8)]

        wklb = [sb(f"wklb{c}", [128, LD], BF16) for c in range(8)]
        wvlb = [sb(f"wvlb{c}", [128, LD], BF16) for c in range(8)]
        wstg2 = sb("wstg2", [128, 2 * LD], F32)
        wffn = sb("wffn", [128, 8 * D], BF16)
        ct = sb("ct", [128, 6], F32)
        idt = sb("idt", [128, 128], BF16)
        mlo = sb("mlo", [128, 1024], BF16)
        mha = sb("mha", [128, 1024], BF16)
        ones_b = sb("ones_b", [1, 128], BF16)
        borow_f = sb("borow_f", [1, D], F32)
        borow = sb("borow", [1, D], BF16)
        birow_f = sb("birow_f", [1, D], F32)
        birow = sb("birow", [1, D], BF16)
        gam_sb = sb("gam_sb", [128, NOWN], F32)
        pm_sb = sb("pm_sb", [128, NCORE], F32)
        qs12 = sb("qs12", [128, GH], F32)
        qr12 = sb("qr12", [128, GH], F32)
        qfm = sb("qfm", [128, 6 * T], BF16)
        vtl = [sb(f"vtl{t}", [128, LD], BF16) for t in range(NTILE)]
        dsum = sb("dsum", [128, 4], F32)
        drc = sb("drc", [128, 4], F32)
        offs = sb("offs", [128, SW], F32)
        sg = [sb(f"sg{i}", [128, SW], F32) for i in range(3)]
        sgo = [sb(f"sgo{i}", [128, SW], F32) for i in range(2)]
        zt6 = sb("zt6", [128, 6], F32)
        rz6 = sb("rz6", [128, 6], F32)
        ctxn = [sb(f"ctxn{i}", [128, 384], BF16) for i in range(2)]
        outf = [sb(f"outf{i}", [128, D], F32) for i in range(2)]

        pall = ctx.enter_context(nc.psum_tensor("pall", [128, 4096], F32))

        s_ct = sem("s_ct")
        s_idt = sem("s_idt")
        s_mlo = sem("s_mlo")
        s_mha = sem("s_mha")
        s_pm = sem("s_pm")
        s_gam = sem("s_gam")
        s_tails = sem("s_tails")
        s_bo = sem("s_bo")
        s_bi = sem("s_bi")
        s_xn = sem("s_xn")
        s_xn2 = sem("s_xn2")
        s_wl = [sem(f"s_wl{i}") for i in range(2)]
        s_wl2 = sem("s_wl2")
        s_wcq = sem("s_wcq")
        s_wco = sem("s_wco")
        s_wclv = sem("s_wclv")
        s_rows = sem("s_rows")
        s_pk = sem("s_pk")
        s_dc = sem("s_dc")
        s_pe = sem("s_pe")
        s_ev = sem("s_ev")
        s_qe = sem("s_qe")
        s_qt = sem("s_qt")
        s_msk = sem("s_msk")
        s_lxe = sem("s_lxe")
        s_att = sem("s_att")
        s_ctxn = sem("s_ctxn")
        s_sgl = [sem(f"s_sgl{i}") for i in range(3)]
        s_sgc = sem("s_sgc")
        s_off = sem("s_off")
        s_xql = [sem(f"s_xql{i}") for i in range(2)]
        s_xrl = [sem(f"s_xrl{i}") for i in range(2)]
        s_f1 = sem("s_f1")
        s_f2 = sem("s_f2")
        s_so = [sem(f"s_so{i}") for i in range(2)]
        s_vr = sem("s_vr")
        s_evv = sem("s_evv")

        sems = {"ev": s_ev, "evv": s_evv, "qe": s_qe, "lxe": s_lxe,
                "f1": s_f1, "f2": s_f2}

        blk = ctx.enter_context(nc.Block())

        C = {"dec": [], "p1q": [], "p1t": [], "p2v": [], "p2g": [], "p3d": [],
             "p3t": [], "p3a": [], "p4": [], "p5f": [], "p5o": []}
        evn = [0]

        def evc():
            evn[0] += 1
            return evn[0]

        @blk.tensor
        def _(tensor):
            ring = _Ring(tensor, pall, sems)
            tensor.wait_ge(s_ct, 16)
            tensor.wait_ge(s_idt, 16)

            for pr in range(8):
                tensor.wait_ge(s_dc, pr + 1)
                for half in range(2):
                    e = evc()
                    bank, idx = ring.claim(1, [("ev", e)])
                    C["dec"].append((bank, idx, e))
                    for q in range(4):
                        ic = 4 * half + q
                        mm = tensor.matmul(
                            out=bank[:, 128 * q:128 * (q + 1)],
                            lhsT=wrow[pr % 2][:, 128 * ic:128 * (ic + 1)],
                            rhs=idt[:], start=True, stop=True)
                    mm.then_inc(s_pe, 1)

            tensor.wait_ge(s_xn, 16 * 8)
            tensor.wait_ge(s_wcq, 8)
            xn2_waited = [False]

            def p1_tr(t2):
                tensor.wait_ge(s_qt, t2 + 1)
                for half in range(2):
                    nt = 4 if half == 0 else 2
                    e = evc()
                    bank, idx = ring.claim(1, [("ev", e)])
                    C["p1t"].append((bank, idx, e))
                    for q in range(nt):
                        r = 4 * half + q
                        mm = tensor.matmul(
                            out=bank[:, 128 * q:128 * (q + 1)],
                            lhsT=qtm[t2 % 2][:, 128 * r:128 * (r + 1)],
                            rhs=idt[:], start=True, stop=True)
                    mm.then_inc(s_pe, 1)

            for t in range(NOWN):
                if t >= 7 and not xn2_waited[0]:
                    tensor.wait_ge(s_xn2, 16 * 8)
                    xn2_waited[0] = True
                bank, idx = ring.claim(2, [("qe", t + 1), ("qe", t + 1)])
                C["p1q"].append((bank, idx, None))
                for c in range(8):
                    st, sp = c == 0, c == 7
                    lhsT = xnr[c][:, HALO + 128 * t:HALO + 128 * (t + 1)]
                    tensor.matmul(out=bank[:, 0:512], lhsT=lhsT,
                                  rhs=wqb[c][:, 256:768], start=st, stop=sp)
                    mm = tensor.matmul(out=bank[:, 512:768], lhsT=lhsT,
                                       rhs=wqb[c][:, 768:1024], start=st, stop=sp)
                mm.then_inc(s_pe, 1)
                if t >= 1:
                    p1_tr(t - 1)
            p1_tr(NOWN - 1)

            tensor.wait_ge(s_xn2, 16 * 8)
            tensor.wait_ge(s_wclv, 16)
            for t in range(NTILE):
                e = evc()
                bank, idx = ring.claim(1, [("ev", e)])
                C["p2v"].append((bank, idx, e))
                for c in range(8):
                    mm = tensor.matmul(out=bank[:, 0:256],
                                       lhsT=xnr[c][:, 128 * t:128 * (t + 1)],
                                       rhs=wvlb[c][:], start=c == 0, stop=c == 7)
                mm.then_inc(s_pe, 1)
            evv_n = [0]
            for g in range(NG):
                n = GSZ[g]
                for part in range(4):
                    o2 = part % 2
                    if part < 2:
                        evv_n[0] += 1
                        bank, idx = ring.claim(1, [("evv", evv_n[0])])
                        C["p2g"].append((bank, idx, None))
                        for c in range(8):
                            mm = tensor.matmul(
                                out=bank[:, 0:n],
                                lhsT=wqb[c][:, 128 * o2:128 * (o2 + 1)],
                                rhs=xnr[c][:, 512 * g:512 * g + n],
                                start=c == 0, stop=c == 7)
                        mm.then_inc(s_pe, 1)
                        continue
                    e = evc()
                    bank, idx = ring.claim(1, [("ev", e)])
                    C["p2g"].append((bank, idx, e))
                    for c in range(8):
                        if part < 2:
                            rhsw = wqb[c][:, 128 * o2:128 * (o2 + 1)]
                        else:
                            rhsw = wklb[c][:, 128 * o2:128 * (o2 + 1)]
                        mm = tensor.matmul(
                            out=bank[:, 0:n],
                            lhsT=rhsw,
                            rhs=xnr[c][:, 512 * g:512 * g + n],
                            start=c == 0, stop=c == 7)
                    mm.then_inc(s_pe, 1)

            def p3_tr(b2):
                tensor.wait_ge(s_att, b2 + 1)
                for half in range(2):
                    e = evc()
                    bank, idx = ring.claim(1, [("ev", e)])
                    C["p3t"].append((bank, idx, e))
                    for q in range(4):
                        r = 4 * half + q
                        mm = tensor.matmul(
                            out=bank[:, 128 * q:128 * (q + 1)],
                            lhsT=attn_[b2 % 2][:, 128 * r:128 * (r + 1)],
                            rhs=idt[:], start=True, stop=True)
                    mm.then_inc(s_pe, 1)

            def p3_av(b2):
                tensor.wait_ge(s_ev, C["p3t"][2 * b2 + 1][2])
                e = evc()
                bank, idx = ring.claim(1, [("ev", e)])
                C["p3a"].append((bank, idx, e))
                for h in range(4):
                    for cc in range(2):
                        sl = 256 * (2 * (h % 2) + h // 2) + 128 * cc
                        mm = tensor.matmul(
                            out=bank[64 * (h % 2):64 * (h % 2) + 64,
                                     128 * (h // 2):128 * (h // 2) + 128],
                            lhsT=vtl[b2 + cc][:, 64 * h:64 * (h + 1)],
                            rhs=attT[b2 % 2][:, sl:sl + 128],
                            start=cc == 0, stop=cc == 1,
                            tile_position=(0, 64 * (h % 2)))
                mm.then_inc(s_pe, 1)

            tensor.wait_ge(s_ev, C["p1t"][-1][2])
            tensor.wait_ge(s_rows, 2)

            def p3_dots(b):
                bank, idx = ring.claim(2, [("lxe", b + 1), ("lxe", b + 1)])
                C["p3d"].append((bank, idx, None))
                for h in (0, 2, 1, 3):
                    colh = 512 * (h % 2) + 256 * (h // 2)
                    mm = tensor.matmul(
                        out=bank[:, colh:colh + 256],
                        lhsT=qfl[h // 2][64 * (h % 2):64 * (h % 2) + 64,
                                         HALO + 128 * b:HALO + 128 * (b + 1)],
                        rhs=kfl[h // 2][64 * (h % 2):64 * (h % 2) + 64,
                                        128 * b:128 * b + 256],
                        start=True, stop=True,
                        tile_position=(64 * (h % 2), 0))
                mm.then_inc(s_pe, 1)

            def p4_mm(u):
                tensor.wait_ge(s_ctxn, u + 1)
                e1, e2 = evc(), evc()
                bank, idx = ring.claim(2, [("ev", e1), ("ev", e2)])
                C["p4"].append((bank, idx, e2))
                for h in range(GH):
                    j = h // 2
                    pr = h % 2
                    mm = tensor.matmul(
                        out=bank[64 * pr:64 * pr + 64,
                                 512 * pr + 64 * j:512 * pr + 64 * (j + 1)],
                        lhsT=ctxn[u % 2][64 * pr:64 * pr + 64,
                                         64 * j:64 * (j + 1)],
                        rhs=qfm[64 * pr:64 * pr + 64,
                                T * j + 64 * u:T * j + 64 * (u + 1)],
                        start=True, stop=True,
                        tile_position=(64 * pr, 64 * pr))
                mm.then_inc(s_pe, 1)

            def p5f_mm(t):
                tensor.wait_ge(s_xql[t % 2], 16 * (t // 2 + 1))
                bankF, idxF = ring.claim(2, [("f2", t + 1), ("f2", t + 1)])
                C["p5f"].append((bankF, idxF, None))
                for c in range(8):
                    st, sp = c == 0, c == 7
                    lhsT = xqs[t % 2][:, 128 * c:128 * (c + 1)]
                    tensor.matmul(out=bankF[:, 0:512], lhsT=lhsT,
                                  rhs=wffn[:, 1024 * c:1024 * c + 512],
                                  start=st, stop=sp)
                    mm = tensor.matmul(out=bankF[:, 512:1024], lhsT=lhsT,
                                       rhs=wffn[:, 1024 * c + 512:1024 * (c + 1)],
                                       start=st, stop=sp)
                mm.then_inc(s_pe, 1)

            def p5o_mm(t):
                if t == 0:
                    tensor.wait_ge(s_wco, 8)
                tensor.wait_ge(s_ev, C["p3a"][t][2])
                tensor.wait_ge(s_ev, C["p4"][2 * t + 1][2])
                bankO, idxO = ring.claim(2, [("f1", t + 1), ("f1", t + 1)])
                C["p5o"].append((bankO, idxO, None))
                for half in range(2):
                    for c in range(8):
                        tensor.matmul(
                            out=bankO[:, 512 * half:512 * (half + 1)],
                            lhsT=afm[:, 2048 * c + 128 * t:2048 * c + 128 * (t + 1)],
                            rhs=wob[c][:, 512 * half:512 * (half + 1)],
                            start=c == 0, stop=False)
                    tensor.matmul(out=bankO[:, 512 * half:512 * (half + 1)],
                                  lhsT=ones_b[:],
                                  rhs=borow[0:1, 512 * half:512 * (half + 1)],
                                  start=False, stop=False)
                    mm = tensor.matmul(
                        out=bankO[:, 512 * half:512 * (half + 1)],
                        lhsT=ones_b[:],
                        rhs=birow[0:1, 512 * half:512 * (half + 1)],
                        start=False, stop=True)
                mm.then_inc(s_pe, 1)

            for b in range(NOWN + 5):
                if b < NOWN:
                    p3_dots(b)
                if 1 <= b <= NOWN:
                    p3_tr(b - 1)
                if 2 <= b <= NOWN + 1:
                    p3_av(b - 2)
                if b < NOWN:
                    p4_mm(2 * b)
                    p4_mm(2 * b + 1)
                t5 = b - 3
                if 0 <= t5 < NOWN:
                    p5f_mm(t5)
                    p5o_mm(t5)

        @blk.vector
        def _(vector):
            vrn = [0]

            def vrt(instr):
                vrn[0] += 1
                instr.then_inc(s_vr, 1)
                vector.wait_ge(s_vr, vrn[0])

            vector.memset(ones_b[:], 1.0)
            vector.wait_ge(s_pk, 32)
            for pr in range(8):
                s, c = divmod(pr, 2)
                if pr >= 2:
                    vector.wait_ge(s_pe, C["dec"][2 * (pr - 2) + 1][1] + 1)
                vrt(vector.tensor_scalar(out=crumb, in0=pkd[c], scalar1=2 * s,
                                         scalar2=3, op0=ALU.logical_shift_right,
                                         op1=ALU.bitwise_and))
                vector.tensor_scalar(out=wrow[pr % 2], in0=crumb,
                                     scalar1=-1.0, scalar2=None,
                                     op0=ALU.add).then_inc(s_dc, 1)
                vector.wait_ge(s_dc, pr + 1)

            vector.wait_ge(s_tails, 16)
            vector.wait_ge(s_pm, 16)
            vrt(vector.tensor_scalar(out=offs[:], in0=tails_sb[:, 0:SW],
                                     scalar1=pm_sb[:, 0:1], scalar2=None,
                                     op0=ALU.mult))
            for j in range(1, NCORE):
                ins = vector.scalar_tensor_tensor(
                    out=offs[:], in0=tails_sb[:, SW * j:SW * (j + 1)],
                    scalar=pm_sb[:, j:j + 1], in1=offs[:], op0=ALU.mult,
                    op1=ALU.add)
                if j == NCORE - 1:
                    ins.then_inc(s_off, 1)
                    vector.wait_ge(s_off, 1)
                else:
                    vrt(ins)

            for t in range(NOWN):
                vector.wait_ge(s_qe, t + 1)
                qe = qe_sb[t % 2]
                vrt(vector.reduce_sum(out=qs12[:],
                                      in_=qe[:].rearrange("p (h d) -> p h d",
                                                          h=GH), axis=AX))
                vrt(vector.reciprocal(out=qr12[:], in_=qs12[:]))
                if t >= 2:
                    vector.wait_ge(s_pe, C["p1t"][2 * (t - 2) + 1][1] + 1)
                vector.tensor_tensor(
                    out=qtm[t % 2][:].rearrange("p (h d) -> p h d", h=GH),
                    in0=qe[:].rearrange("p (h d) -> p h d", h=GH),
                    in1=qr12[:].unsqueeze(2).broadcast_to([128, GH, DH]),
                    op=ALU.mult).then_inc(s_qt, 1)

            gi2 = 0
            for g in range(NG):
                n = GSZ[g]
                for part in range(4):
                    bank, idx, _e = C["p2g"][gi2]
                    gi2 += 1
                    if part >= 2:
                        continue
                    vector.wait_ge(s_pe, idx + 1)
                    vector.tensor_scalar(out=qfl[part % 2][:, 512 * g:512 * g + n],
                                         in0=bank[:, 0:n], scalar1=0.125,
                                         scalar2=None,
                                         op0=ALU.mult).then_inc(s_evv, 1)

            vector.wait_ge(s_mlo, 16)
            vector.wait_ge(s_mha, 16)
            vector.wait_ge(s_gam, 16)
            for b in range(NOWN + 5):
                if b < NOWN:
                    dbank = C["p3d"][b][0]
                    vector.wait_ge(s_pe, C["p3d"][b][1] + 1)
                    if b == 0:
                        vrt(vector.tensor_tensor(out=dbank[:, 0:1024],
                                                 in0=dbank[:, 0:1024],
                                                 in1=mha[:], op=ALU.add))
                    vector.tensor_tensor(out=dbank[:, 0:1024],
                                         in0=dbank[:, 0:1024],
                                         in1=mlo[:], op=ALU.add).then_inc(s_msk, 1)
                    vector.wait_ge(s_lxe, b + 1)
                    vrt(vector.reciprocal(out=drc[:], in_=dsum[:]))
                    if b >= 2:
                        vector.wait_ge(s_pe, C["p3t"][2 * (b - 2) + 1][1] + 1)
                    vector.tensor_tensor(
                        out=attn_[b % 2][:].rearrange("p (e f) -> p e f", e=4),
                        in0=attw[b % 2][:].rearrange("p (e f) -> p e f", e=4),
                        in1=drc[:].unsqueeze(2).broadcast_to([128, 4, 256]),
                        op=ALU.mult).then_inc(s_att, 1)
                if b < NOWN:
                    for u in (2 * b, 2 * b + 1):
                        vector.wait_ge(s_sgl[u % 3], 16 * (u // 3 + 1))
                        vector.tensor_tensor(out=sgo[u % 2][:], in0=sg[u % 3][:],
                                             in1=offs[:],
                                             op=ALU.add).then_inc(s_sgc, 1)
                        vector.wait_ge(s_sgc, u + 1)
                        sz = sgo[u % 2][:].rearrange("p (j w) -> p j w", j=6)
                        vrt(vector.tensor_scalar(out=zt6[:], in0=sz[:, :, 64],
                                                 scalar1=1e-6, scalar2=None,
                                                 op0=ALU.add))
                        vrt(vector.reciprocal(out=rz6[:], in_=zt6[:]))
                        if u >= 2:
                            vector.wait_ge(s_pe, C["p4"][u - 2][1] + 1)
                        vector.tensor_tensor(
                            out=ctxn[u % 2][:].rearrange("p (j w) -> p j w", j=6),
                            in0=sz[:, :, 0:64],
                            in1=rz6[:].unsqueeze(2).broadcast_to([128, 6, 64]),
                            op=ALU.mult).then_inc(s_ctxn, 1)
                t5 = b - 3
                if 0 <= t5 < NOWN:
                    t = t5
                    vector.wait_ge(s_xrl[t % 2], 16 * (t // 2 + 1))
                    vector.wait_ge(s_pe, C["p5o"][t][1] + 1)
                    if t >= 1:
                        vector.wait_ge(s_f2, t)
                    vector.scalar_tensor_tensor(
                        out=tmpf, in0=C["p5o"][t][0], scalar=1.0, in1=xrs[t % 2],
                        op0=ALU.mult, op1=ALU.add).then_inc(s_f1, 1)
                    vector.wait_ge(s_f1, t + 1)
                    vector.wait_ge(s_pe, C["p5f"][t][1] + 1)
                    if t >= 2:
                        vector.wait_ge(s_so[t % 2], 16 * (t // 2))
                    vector.scalar_tensor_tensor(
                        out=outf[t % 2][:], in0=C["p5f"][t][0],
                        scalar=gam_sb[:, t:t + 1], in1=tmpf, op0=ALU.mult,
                        op1=ALU.add).then_inc(s_f2, 1)

        @blk.scalar
        def _(scalar):
            for c in range(8):
                scalar.wait_ge(s_wl[c % 2], 16 * (c // 2 + 1))
                scalar.copy(out=wqb[c][:], in_=wstg[c % 2][:]).then_inc(s_wcq, 1)
                scalar.wait_ge(s_wcq, c + 1)
            scalar.wait_ge(s_bo, 16)
            scalar.wait_ge(s_bi, 16)
            scalar.copy(out=borow[:], in_=borow_f[:]).then_inc(s_rows, 1)
            scalar.copy(out=birow[:], in_=birow_f[:]).then_inc(s_rows, 1)

            for pr in range(8):
                for half in range(2):
                    bank, idx, _e = C["dec"][2 * pr + half]
                    scalar.wait_ge(s_pe, idx + 1)
                    s, c = divmod(pr, 2)
                    col = 256 * s + 128 * c
                    scalar.copy(
                        out=wffn[:].rearrange("p (i o) -> p i o", i=8)[
                            :, 4 * half:4 * half + 4, col:col + 128],
                        in_=bank[:].rearrange("p (i o) -> p i o", i=4)
                    ).then_inc(s_ev, 1)

            def p1_ev(t2):
                for half in range(2):
                    bank, idx, _e = C["p1t"][2 * t2 + half]
                    scalar.wait_ge(s_pe, idx + 1)
                    nt = 4 if half == 0 else 2
                    scalar.copy(
                        out=qfm[:].rearrange("p (r t) -> p r t", r=6)[
                            :, 4 * half:4 * half + nt, 128 * t2:128 * (t2 + 1)],
                        in_=bank[:, 0:128 * nt].rearrange("p (r t) -> p r t",
                                                          r=nt)
                    ).then_inc(s_ev, 1)

            scalar.wait_ge(s_ct, 16)
            for t in range(NOWN):
                bank, idx, _e = C["p1q"][t]
                scalar.wait_ge(s_pe, idx + 1)
                if t >= 2:
                    scalar.wait_ge(s_qt, t - 1)
                scalar.activation(out=qe_sb[t % 2][:], in_=bank[:, 0:GD],
                                  func=AF.Exp, bias=ct[:, 3:4],
                                  scale=1.0).then_inc(s_qe, 1)
                if t < 8:
                    c = t
                    scalar.wait_ge(s_wl2, 32 * (c + 1))
                    scalar.copy(out=wklb[c][:],
                                in_=wstg2[:, 0:LD]).then_inc(s_wclv, 1)
                    scalar.copy(out=wvlb[c][:],
                                in_=wstg2[:, LD:2 * LD]).then_inc(s_wclv, 1)
                    scalar.wait_ge(s_wclv, 2 * (c + 1))
                if t >= 1:
                    p1_ev(t - 1)
            p1_ev(NOWN - 1)

            for t in range(NTILE):
                bank, idx, _e = C["p2v"][t]
                scalar.wait_ge(s_pe, idx + 1)
                scalar.copy(out=vtl[t][:], in_=bank[:, 0:256]).then_inc(s_ev, 1)
            gi = 0
            for g in range(NG):
                n = GSZ[g]
                for part in range(4):
                    bank, idx, _e = C["p2g"][gi]
                    gi += 1
                    if part < 2:
                        continue
                    scalar.wait_ge(s_pe, idx + 1)
                    dst = kfl[part % 2]
                    scalar.copy(out=dst[:, 512 * g:512 * g + n],
                                in_=bank[:, 0:n]).then_inc(s_ev, 1)

            def p3_ev(b2):
                if b2 >= 2:
                    scalar.wait_ge(s_pe, C["p3a"][b2 - 2][1] + 1)
                for half in range(2):
                    bank, idx, _e = C["p3t"][2 * b2 + half]
                    scalar.wait_ge(s_pe, idx + 1)
                    scalar.copy(out=attT[b2 % 2][:, 512 * half:512 * (half + 1)],
                                in_=bank[:]).then_inc(s_ev, 1)

            def p3_aev(b2):
                bank, idx, _e = C["p3a"][b2]
                scalar.wait_ge(s_pe, idx + 1)
                scalar.copy(
                    out=afm[:].rearrange("p (g t) -> p g t", g=8)[
                        :, 0:2, 128 * b2:128 * (b2 + 1)],
                    in_=bank[:, 0:256].rearrange("p (g t) -> p g t", g=2)
                ).then_inc(s_ev, 1)

            def p4_ev(u):
                bank, idx, _e = C["p4"][u]
                scalar.wait_ge(s_pe, idx + 1)
                a3 = afm[:].rearrange("p (g t) -> p g t", g=8)
                scalar.copy(out=a3[0:64, 2:8, 64 * u:64 * (u + 1)],
                            in_=bank[0:64, 0:384].rearrange(
                                "p (j w) -> p j w", j=6)).then_inc(s_ev, 1)
                scalar.copy(out=a3[64:128, 2:8, 64 * u:64 * (u + 1)],
                            in_=bank[64:128, 512:896].rearrange(
                                "p (j w) -> p j w", j=6)).then_inc(s_ev, 1)

            for b in range(NOWN + 2):
                if b < NOWN:
                    scalar.wait_ge(s_msk, b + 1)
                    if b >= 1:
                        scalar.wait_ge(s_att, b)
                    dbank = C["p3d"][b][0]
                    for e in range(4):
                        ins = scalar.activation(
                            out=attw[b % 2][:, 256 * e:256 * (e + 1)],
                            in_=dbank[:, 256 * e:256 * (e + 1)], func=AF.Exp,
                            bias=ct[:, 2:3], scale=1.0,
                            accum_out=dsum[:, e:e + 1])
                        if e == 3:
                            ins.then_inc(s_lxe, 1)
                if b < 2:
                    for c in range(4 * b, 4 * b + 4):
                        if c == 0:
                            scalar.wait_ge(s_pe, C["p2g"][-1][1] + 1)
                        scalar.wait_ge(s_wl[c % 2], 16 * (c // 2 + 5))
                        scalar.copy(out=wob[c][:],
                                    in_=wstg[c % 2][:]).then_inc(s_wco, 1)
                        scalar.wait_ge(s_wco, c + 1)
                if 1 <= b <= NOWN:
                    p3_ev(b - 1)
                if 2 <= b <= NOWN + 1:
                    p3_aev(b - 2)
                if b < NOWN:
                    p4_ev(2 * b)
                    p4_ev(2 * b + 1)

        @blk.sync
        def _(sync):
            sync.dma_start(out=ct[:], in_=csts).then_inc(s_ct, 16)
            sync.dma_start(out=idt[:], in_=idn).then_inc(s_idt, 16)
            sync.dma_start(out=mlo[:], in_=mloc_i).then_inc(s_mlo, 16)
            sync.dma_start(out=mha[:], in_=mhal_i).then_inc(s_mha, 16)
            sync.dma_start(out=pm_sb[:], in_=pm_i).then_inc(s_pm, 16)
            sync.dma_start(out=gam_sb[:], in_=gam_i).then_inc(s_gam, 16)
            sync.dma_start(out=tails_sb[:].rearrange("p (j c) -> p j c",
                                                     j=NCORE),
                           in_=tails_i.rearrange("j p c -> p j c")
                           ).then_inc(s_tails, 16)
            sync.dma_start(out=borow_f[:], in_=bo_i).then_inc(s_bo, 16)
            sync.dma_start(out=birow_f[:], in_=bias_i).then_inc(s_bi, 16)
            for c in range(8):
                sync.dma_start(out=xnr[c][:, 0:1088],
                               in_=xnfm_i[128 * c:128 * (c + 1), 0:1088]
                               ).then_inc(s_xn, 16)
            for c in range(8):
                sync.dma_start(out=xnr[c][:, 1088:TT],
                               in_=xnfm_i[128 * c:128 * (c + 1), 1088:TT]
                               ).then_inc(s_xn2, 16)
            for u in range(NCHUNK):
                if u >= 3:
                    sync.wait_ge(s_sgc, u - 2)
                sync.dma_start(out=sg[u % 3][:],
                               in_=sloc_i[u]).then_inc(s_sgl[u % 3], 16)

        @blk.gpsimd
        def _(gpsimd):
            gpsimd.dma_start(out=pkd[0], in_=packed_i[0:128, :]).then_inc(s_pk, 16)
            gpsimd.dma_start(out=pkd[1],
                             in_=packed_i[128:256, :]).then_inc(s_pk, 16)
            for c in range(8):
                if c >= 2:
                    gpsimd.wait_ge(s_wcq, c - 1)
                gpsimd.dma_start(out=wstg[c % 2][:],
                                 in_=wq_i[128 * c:128 * (c + 1), :]
                                 ).then_inc(s_wl[c % 2], 16)
            for c in range(8):
                if c >= 1:
                    gpsimd.wait_ge(s_wclv, 2 * c)
                gpsimd.dma_start(out=wstg2[:, 0:LD],
                                 in_=wkl_i[128 * c:128 * (c + 1), :]
                                 ).then_inc(s_wl2, 16)
                gpsimd.dma_start(out=wstg2[:, LD:2 * LD],
                                 in_=wvl_i[128 * c:128 * (c + 1), :]
                                 ).then_inc(s_wl2, 16)
            gpsimd.wait_ge(s_wcq, 8)
            for c in range(8):
                if c >= 2:
                    gpsimd.wait_ge(s_wco, c - 1)
                gpsimd.dma_start(out=wstg[c % 2][:],
                                 in_=wo_i[128 * c:128 * (c + 1), :]
                                 ).then_inc(s_wl[c % 2], 16)
            gpsimd.wait_ge(s_off, 1)
            gpsimd.wait_ge(s_wco, 8)
            def p5_store(t2):
                gpsimd.wait_ge(s_f2, t2 + 1)
                gpsimd.dma_start(out=out_o[128 * t2:128 * (t2 + 1), :],
                                 in_=outf[t2 % 2][:]).then_inc(s_so[t2 % 2], 16)

            for b in range(NOWN + 6):
                t = b - 2
                if 0 <= t < NOWN:
                    if t >= 2:
                        gpsimd.wait_ge(s_pe, C["p5f"][t - 2][1] + 1)
                        gpsimd.wait_ge(s_f1, t - 1)
                    gpsimd.dma_start(
                        out=xqs[t % 2].rearrange("p (c t) -> p c t", c=8),
                        in_=xqfm_i[:, HALO + 128 * t:HALO + 128 * (t + 1)
                                   ].rearrange("(c p) t -> p c t", p=128)
                        ).then_inc(s_xql[t % 2], 16)
                    gpsimd.dma_start(out=xrs[t % 2],
                                     in_=xown[128 * t:128 * (t + 1), :]
                                     ).then_inc(s_xrl[t % 2], 16)
                ts = b - 4
                if 0 <= ts < NOWN:
                    p5_store(ts)
            gpsimd.wait_ge(s_so[0], 16 * 8)
            gpsimd.wait_ge(s_so[1], 16 * 8)

    return nc


_CACHE = {}


def _get_l1():
    if "l1" not in _CACHE:
        _CACHE["l1"] = build_l1()
    return _CACHE["l1"]


def run_l1(x_full, wk, wv, w_scale):
    import ml_dtypes
    xpad = np.concatenate([np.zeros((HALO, D), np.float32), x_full], axis=0)
    wkg = np.ascontiguousarray(wk[:, 256:])
    wvg = np.ascontiguousarray(wv[:, 256:])
    csts = np.zeros((128, 6), np.float32)
    csts[:, 0] = 1e-5
    csts[:, 1] = w_scale
    csts[:, 3] = NEGC
    csts[:, 4] = C23
    idn = np.eye(128, dtype=ml_dtypes.bfloat16)
    in_maps = []
    for c in range(NCORE):
        in_maps.append({
            "xsl": np.ascontiguousarray(xpad[T * c:T * c + TT]),
            "wkg": wkg, "wvg": wvg, "csts": csts, "idn": idn,
        })
    nc = _get_l1()
    res = run_bass_kernel_spmd(nc, in_maps, core_ids=list(range(NCORE)))
    return res.results


def _get_l2():
    if "l2" not in _CACHE:
        _CACHE["l2"] = build_l2()
    return _CACHE["l2"]


def make_l2_inmaps(inputs, res1):
    import ml_dtypes
    x = np.asarray(inputs["L0_attn_x"][0], np.float32)
    wq = np.asarray(inputs["L0_attn_Wq"], np.float32)
    wo = np.asarray(inputs["L0_attn_Wo"], np.float32)
    wk = np.asarray(inputs["L0_attn_Wk"], np.float32)
    wv = np.asarray(inputs["L0_attn_Wv"], np.float32)
    bo = np.asarray(inputs["L0_attn_bo"], np.float32).reshape(1, D)
    bias = np.asarray(inputs["L0_ffn_bias"], np.float32).reshape(1, D)
    packed = np.ascontiguousarray(
        np.asarray(inputs["L0_ffn_packed_weight"], np.int32).reshape(256, D))
    wkl = np.ascontiguousarray(wk[:, 0:LD])
    wvl = np.ascontiguousarray(wv[:, 0:LD])
    idn = np.eye(128, dtype=ml_dtypes.bfloat16)
    csts = np.zeros((128, 6), np.float32)
    csts[:, 3] = NEGC
    csts[:, 5] = 0.125
    NEGINF = np.float32(-1e30)
    m256 = np.full((128, 256), NEGC, np.float32)
    q_idx = np.arange(128)[:, None]
    k_idx = np.arange(128)[None, :]
    m256[:, 128:256] += np.where(k_idx > q_idx, NEGINF, np.float32(0.0))
    mloc = np.tile(m256, (1, 4)).astype(ml_dtypes.bfloat16)
    mhal0 = np.zeros((128, 1024), np.float32)
    for e in range(4):
        mhal0[:, 256 * e:256 * e + 128] = NEGINF
    tails = np.ascontiguousarray(
        np.stack([np.asarray(res1[c]["states"])[NCHUNK] for c in range(NCORE)], 0))
    in_maps = []
    for c in range(NCORE):
        pm = np.zeros((128, NCORE), np.float32)
        pm[:, :c] = 1.0
        mhal = (mhal0 if c == 0 else np.zeros((128, 1024), np.float32)
                ).astype(ml_dtypes.bfloat16)
        in_maps.append({
            "xown": np.ascontiguousarray(x[T * c:T * (c + 1)]),
            "xnfm": np.asarray(res1[c]["xnfm"]),
            "xqfm": np.asarray(res1[c]["xqfm"]),
            "gam": np.asarray(res1[c]["gam"]),
            "sloc": np.asarray(res1[c]["states"]),
            "tails": tails, "pmask": pm,
            "wq": wq, "wo": wo, "wkl": wkl, "wvl": wvl,
            "bo": bo, "bias": bias, "packed": packed,
            "csts": csts, "mloc": mloc, "mhal": mhal, "idn": idn,
        })
    return in_maps


def kernel(**inputs):
    x = np.asarray(inputs["L0_attn_x"][0], np.float32)
    wk = np.asarray(inputs["L0_attn_Wk"], np.float32)
    wv = np.asarray(inputs["L0_attn_Wv"], np.float32)
    scale = float(np.asarray(inputs["L0_ffn_weight_scale"]).reshape(-1)[0])
    res1 = run_l1(x, wk, wv, scale)
    in_maps = make_l2_inmaps(inputs, res1)
    nc2 = _get_l2()
    res2 = run_bass_kernel_spmd(nc2, in_maps, core_ids=list(range(NCORE)))
    out = np.concatenate([np.asarray(res2.results[c]["out"])
                          for c in range(NCORE)], axis=0)
    return out.reshape(1, 16384, D)


# version 05
# speedup vs baseline: 1.3201x; 1.3201x over previous
import contextlib
import numpy as np
import concourse.bass as bass
import concourse.mybir as mybir
from concourse.bass_utils import run_bass_kernel_spmd

F32 = mybir.dt.float32
BF16 = mybir.dt.bfloat16
I8 = mybir.dt.int8
I32 = mybir.dt.int32
AF = mybir.ActivationFunctionType
ALU = mybir.AluOpType
AX = mybir.AxisListType.X

NCORE = 8
D = 1024
T = 2048
HALO = 128
TT = T + HALO
NTILE = TT // 128
NOWN = T // 128
DH = 64
GH = 12
GD = GH * DH
NCHUNK = T // 64
SW = 390
NEGC = -12.0
C23 = 12582912.0
NPOOL = 4


NWAVE = 28


def _l1_sched():
    uses = []
    for w in range(NWAVE):
        if 0 <= w - 6 < NTILE:
            uses.append(("t34", w - 6))
            uses.append(("t34x", w - 6))
        if 0 <= w - 5 < NTILE:
            uses.append(("t12", w - 5))
            uses.append(("t12x", w - 5))
        if 0 <= w - 10 < NOWN:
            uses.append(("cx", 2 * (w - 10)))
            uses.append(("cx", 2 * (w - 10) + 1))
    evn = 0
    waits, banks, evcnt = [], [], {}
    last = {}
    for n, use in enumerate(uses):
        b = n % NPOOL
        if use[0] != "cx":
            evn += 1
            evcnt[use] = evn
        if b in last:
            kind = last[b]
            if kind[0] == "cx":
                waits.append(("su", kind[1] + 1))
            else:
                waits.append(("ev", evcnt[kind]))
        else:
            waits.append(None)
        banks.append(b)
        last[b] = use
    return uses, banks, waits, evcnt



def build_l1(detect=False, stage=9):
    nc = bass.Bass("TRN2", target_bir_lowering=False, debug=False,
                   detect_race_conditions=detect)

    xsl = nc.dram_tensor("xsl", [TT, D], F32, kind="ExternalInput").ap()
    wkg = nc.dram_tensor("wkg", [D, GD], F32, kind="ExternalInput").ap()
    wvg = nc.dram_tensor("wvg", [D, GD], F32, kind="ExternalInput").ap()
    csts = nc.dram_tensor("csts", [128, 6], F32, kind="ExternalInput").ap()
    idn = nc.dram_tensor("idn", [128, 128], BF16, kind="ExternalInput").ap()

    xnfm = nc.dram_tensor("xnfm", [D, TT], BF16, kind="ExternalOutput").ap()
    xqfm = nc.dram_tensor("xqfm", [D, TT], BF16, kind="ExternalOutput").ap()
    gam = nc.dram_tensor("gam", [128, NOWN], F32, kind="ExternalOutput").ap()
    states = nc.dram_tensor("states", [NCHUNK + 1, 128, SW], F32,
                            kind="ExternalOutput").ap()
    dbg = nc.dram_tensor("dbg", [128, 8 * NTILE], F32, kind="ExternalOutput").ap()

    uses, banks, uwaits, evcnt = _l1_sched()
    ub, uw = {}, {}
    for n, use in enumerate(uses):
        ub[use] = banks[n]
        uw[use] = uwaits[n]
    NGRP = 5
    gtiles = [list(range(4 * g, min(4 * g + 4, NTILE))) for g in range(NGRP)]
    mathwave = [4 * g + 3 for g in range(4)] + [17]

    ctx = contextlib.ExitStack()
    with ctx:
        sb = lambda name, shape, dt: ctx.enter_context(nc.sbuf_tensor(name, shape, dt))
        sem = lambda name: ctx.enter_context(nc.semaphore(name))

        xb1 = [sb(f"xb1{i}", [128, D], F32) for i in range(3)]
        xb2 = [sb(f"xb2{i}", [128, D], F32) for i in range(2)]
        ct = sb("ct", [128, 6], F32)
        idt = sb("idt", [128, 128], BF16)
        sqd = sb("sqd", [128, D], F32)
        asq = sb("asq", [128, D], F32)
        sum_a = sb("sum_a", [128, NTILE], F32)
        ssq_a = sb("ssq_a", [128, NTILE], F32)
        mean_a = sb("mean_a", [128, NTILE], F32)
        var_a = sb("var_a", [128, NTILE], F32)
        sqa_a = sb("sqa_a", [128, NTILE], F32)
        rstd_a = sb("rstd_a", [128, NTILE], F32)
        negmr_a = sb("negmr_a", [128, NTILE], F32)
        xmax_a = sb("xmax_a", [128, NTILE], F32)
        xmin_a = sb("xmin_a", [128, NTILE], F32)
        tmax_a = sb("tmax_a", [128, NTILE], F32)
        tmin_a = sb("tmin_a", [128, NTILE], F32)
        tmpb_a = sb("tmpb_a", [128, NTILE], F32)
        amax_a = sb("amax_a", [128, NTILE], F32)
        tmp2_a = sb("tmp2_a", [128, NTILE], F32)
        rr_a = sb("rr_a", [128, NTILE], F32)
        asc_a = sb("asc_a", [128, NTILE], F32)
        gam_a = sb("gam_a", [128, NTILE], F32)

        sca2_a = sb("sca2_a", [128, NTILE], F32)
        nb2c_a = sb("nb2c_a", [128, NTILE], F32)
        xq_f = [sb(f"xq_f{i}", [128, D], F32) for i in range(2)]
        xn_b = [sb(f"xn_b{i}", [128, D], BF16) for i in range(2)]
        xq_i = [sb(f"xq_i{i}", [128, D], I8) for i in range(2)]
        xq_b = [sb(f"xq_b{i}", [128, D], BF16) for i in range(2)]
        xn_st = [sb(f"xn_st{i}", [128, D], BF16) for i in range(2)]
        xq_st = [sb(f"xq_st{i}", [128, D], BF16) for i in range(2)]

        wk_f = [sb(f"wk_f{i}", [128, GD], F32) for i in range(2)]
        wv_f = [sb(f"wv_f{i}", [128, GD], F32) for i in range(2)]
        wkb = [sb(f"wkb{c}", [128, GD], BF16) for c in range(8)]
        wvb = [sb(f"wvb{c}", [128, GD], BF16) for c in range(8)]

        etmp = sb("etmp", [128, GD], F32)
        kmx = [sb(f"kmx{i}", [128, GD], F32) for i in range(2)]
        eexp = [sb(f"eexp{i}", [128, GD], F32) for i in range(2)]
        ktm = [sb(f"ktm{j}", [128, GD], BF16) for j in range(NOWN)]
        vtm = [sb(f"vtm{j}", [128, 780], BF16) for j in range(NOWN)]
        st_ring = sb("st_ring", [128, 3 * SW], F32)

        pool = [ctx.enter_context(nc.psum_tensor(f"pp{i}", [128, 512], F32))
                for i in range(NPOOL)]
        kvt = ctx.enter_context(nc.psum_tensor("kvt", [128, 2048], F32))
        k_all = kvt[:, 0:GD]
        v_all = kvt[:, 1024:1024 + GD]

        s_c = sem("s_c")
        s_x1 = [sem(f"s_x1_{i}") for i in range(3)]
        s_x2 = [sem(f"s_x2_{i}") for i in range(2)]
        s_st1 = sem("s_st1")
        s_sa1 = sem("s_sa1")
        s_ini = sem("s_ini")
        s_vvg = sem("s_vvg")
        s_sqg = sem("s_sqg")
        s_smg = sem("s_smg")
        s_wlk = [sem(f"s_wlk{i}") for i in range(2)]
        s_wlv = [sem(f"s_wlv{i}") for i in range(2)]
        s_wc = sem("s_wc")
        s_xnb = sem("s_xnb")
        s_xqf = sem("s_xqf")
        s_xqi = sem("s_xqi")
        s_qb = sem("s_qb")
        s_t12 = sem("s_t12")
        s_t34 = sem("s_t34")
        s_ev = sem("s_ev")
        s_soxn = [sem(f"s_soxn{i}") for i in range(2)]
        s_soxq = [sem(f"s_soxq{i}") for i in range(2)]
        s_mm = sem("s_mm")
        s_et = sem("s_et")
        s_km = sem("s_km")
        s_ee = sem("s_ee")
        s_kv = sem("s_kv")
        s_vv = sem("s_vv")
        s_cx = sem("s_cx")
        s_su = sem("s_su")
        s_sd = [sem(f"s_sd{i}") for i in range(3)]
        s_fo = sem("s_fo")
        s_vr = sem("s_vr")

        blk = ctx.enter_context(nc.Block())

        def so_cnt(i):
            return 16 * ((i - (i % 2)) // 2 + 1)

        @blk.sync
        def _(sync):
            sync.dma_start(out=ct[:], in_=csts).then_inc(s_c, 16)
            sync.dma_start(out=idt[:], in_=idn).then_inc(s_c, 16)
            for w in range(NWAVE):
                if w < NTILE:
                    i = w
                    if i >= 3:
                        sync.wait_ge(s_st1, i - 2)
                        sync.wait_ge(s_sa1, i - 2)
                    sync.dma_start(out=xb1[i % 3][:],
                                   in_=xsl[128 * i:128 * (i + 1), :]
                                   ).then_inc(s_x1[i % 3], 16)
                if w < 8:
                    c = w
                    if c >= 2:
                        sync.wait_ge(s_wc, 2 * c - 3)
                    sync.dma_start(out=wk_f[c % 2][:],
                                   in_=wkg[128 * c:128 * (c + 1), :]
                                   ).then_inc(s_wlk[c % 2], 16)
                    if c >= 2:
                        sync.wait_ge(s_wc, 2 * c - 2)
                    sync.dma_start(out=wv_f[c % 2][:],
                                   in_=wvg[128 * c:128 * (c + 1), :]
                                   ).then_inc(s_wlv[c % 2], 16)
                i2 = w - 2
                if 0 <= i2 < NTILE:
                    if i2 >= 2:
                        sync.wait_ge(s_xnb, i2 - 1)
                    sync.dma_start(out=xb2[i2 % 2][:],
                                   in_=xsl[128 * i2:128 * (i2 + 1), :]
                                   ).then_inc(s_x2[i2 % 2], 16)

        @blk.vector
        def _(vector):
            vrn = [0]

            def vrt(instr):
                vrn[0] += 1
                instr.then_inc(s_vr, 1)
                vector.wait_ge(s_vr, vrn[0])

            for j in range(NOWN):
                vector.memset(
                    vtm[j][:].rearrange("p (h c) -> p h c", h=GH)[:, :, 64:65], 1.0)
            vector.memset(st_ring[:, 0:SW], 0.0).then_inc(s_ini, 1)

            def v_math(g):
                c0, c1 = 4 * g, min(4 * g + 4, NTILE)
                sl = slice(c0, c1)
                vector.wait_ge(s_sa1, c1)
                vrt(vector.tensor_scalar(out=mean_a[:, sl], in0=sum_a[:, sl],
                                         scalar1=1.0 / D, scalar2=None,
                                         op0=ALU.mult))
                vector.tensor_tensor(out=tmpb_a[:, sl], in0=mean_a[:, sl],
                                     in1=mean_a[:, sl], op=ALU.mult)
                vector.tensor_tensor(out=tmax_a[:, sl], in0=xmax_a[:, sl],
                                     in1=mean_a[:, sl], op=ALU.subtract)
                vrt(vector.tensor_tensor(out=tmin_a[:, sl], in0=mean_a[:, sl],
                                         in1=xmin_a[:, sl], op=ALU.subtract))
                vector.scalar_tensor_tensor(out=var_a[:, sl], in0=ssq_a[:, sl],
                                            scalar=1.0 / D, in1=tmpb_a[:, sl],
                                            op0=ALU.mult, op1=ALU.subtract)
                vector.tensor_tensor(out=amax_a[:, sl], in0=tmax_a[:, sl],
                                     in1=tmin_a[:, sl],
                                     op=ALU.max).then_inc(s_vvg, 1)
                vector.wait_ge(s_sqg, g + 1)
                vrt(vector.reciprocal(out=rstd_a[:, sl], in_=sqa_a[:, sl]))
                vector.scalar_tensor_tensor(out=negmr_a[:, sl],
                                            in0=mean_a[:, sl], scalar=-1.0,
                                            in1=rstd_a[:, sl], op0=ALU.mult,
                                            op1=ALU.mult)
                vrt(vector.tensor_tensor(out=amax_a[:, sl], in0=amax_a[:, sl],
                                         in1=rstd_a[:, sl], op=ALU.mult))
                vrt(vector.tensor_scalar(out=tmp2_a[:, sl], in0=amax_a[:, sl],
                                         scalar1=1e-5, scalar2=None,
                                         op0=ALU.max))
                vrt(vector.reciprocal(out=rr_a[:, sl], in_=tmp2_a[:, sl]))
                vrt(vector.tensor_scalar(out=asc_a[:, sl], in0=rr_a[:, sl],
                                         scalar1=127.0, scalar2=None,
                                         op0=ALU.mult))
                vector.tensor_tensor(out=sca2_a[:, sl], in0=asc_a[:, sl],
                                     in1=rstd_a[:, sl], op=ALU.mult)
                vector.tensor_tensor(out=nb2c_a[:, sl], in0=asc_a[:, sl],
                                     in1=negmr_a[:, sl], op=ALU.mult)
                vector.tensor_scalar(out=gam_a[:, sl], in0=tmp2_a[:, sl],
                                     scalar1=1.0 / 127.0, scalar2=ct[:, 1:2],
                                     op0=ALU.mult, op1=ALU.mult).then_inc(s_smg, 1)

            for w in range(NWAVE):
                i6v = w - 6
                if 0 <= i6v < NTILE:
                    vector.wait_ge(s_xqi, i6v + 1)
                    if i6v >= 2:
                        vector.wait_ge(s_t34, 2 * (i6v - 2) + 2)
                    vector.tensor_copy(out=xq_b[i6v % 2][:],
                                       in_=xq_i[i6v % 2][:]).then_inc(s_qb, 1)
                j3 = w - 11
                if 0 <= j3 < NOWN:
                    for s in range(2):
                        u = 2 * j3 + s
                        vector.wait_ge(s_cx, u + 1)
                        if u >= 2:
                            vector.wait_ge(s_sd[(u + 1) % 3],
                                           16 * ((u - 2) // 3 + 1))
                        vector.tensor_tensor(
                            out=st_ring[:, SW * ((u + 1) % 3):SW * ((u + 1) % 3) + SW],
                            in0=st_ring[:, SW * (u % 3):SW * (u % 3) + SW],
                            in1=pool[ub[("cx", u)]][:, 0:SW],
                            op=ALU.add).then_inc(s_su, 1)
                        vector.wait_ge(s_su, u + 1)
                j8 = w - 8
                if 0 <= j8 < NOWN:
                    vector.wait_ge(s_mm, j8 + 1)
                    if j8 >= 1:
                        vector.wait_ge(s_ee, j8)
                    vector.tensor_scalar(out=etmp[:], in0=k_all, scalar1=0.0,
                                         scalar2=None,
                                         op0=ALU.min).then_inc(s_et, 1)
                    vector.tensor_scalar(out=kmx[j8 % 2][:], in0=k_all,
                                         scalar1=0.0, scalar2=None,
                                         op0=ALU.max).then_inc(s_km, 1)
                if w < NTILE:
                    i = w
                    vector.wait_ge(s_x1[i % 3], 16 * (i // 3 + 1))
                    if i >= 1:
                        vector.wait_ge(s_st1, i)
                    x_ap = xb1[i % 3][:]
                    vector.reduce_max(out=xmax_a[:, i:i + 1], in_=x_ap, axis=AX)
                    vector.tensor_reduce(out=xmin_a[:, i:i + 1], in_=x_ap,
                                         axis=AX, op=ALU.min)
                    vector.scalar_tensor_tensor(
                        out=sqd[:], in0=x_ap, scalar=1.0, in1=x_ap,
                        op0=ALU.mult, op1=ALU.mult,
                        accum_out=ssq_a[:, i:i + 1]).then_inc(s_st1, 1)
                if w in mathwave:
                    g = mathwave.index(w)
                    vector.wait_ge(s_c, 32)
                    v_math(g)
                i5 = w - 5
                if 0 <= i5 < NTILE:
                    vector.wait_ge(s_xqf, i5 + 1)
                    if i5 >= 2:
                        vector.wait_ge(s_qb, i5 - 1)
                    vector.tensor_scalar(out=xq_i[i5 % 2][:],
                                         in0=xq_f[i5 % 2][:], scalar1=C23,
                                         scalar2=C23, op0=ALU.add,
                                         op1=ALU.subtract).then_inc(s_xqi, 1)
                j9 = w - 9
                if 0 <= j9 < NOWN:
                    vector.wait_ge(s_ee, j9 + 1)
                    vector.tensor_tensor(out=ktm[j9][:], in0=kmx[j9 % 2][:],
                                         in1=eexp[j9 % 2][:],
                                         op=ALU.add).then_inc(s_kv, 1)

        @blk.scalar
        def _(scalar):
            scalar.wait_ge(s_c, 32)
            for w in range(NWAVE):
                i6 = w - 6
                j8 = w - 8
                if 0 <= j8 < NOWN:
                    scalar.wait_ge(s_mm, j8 + 1)
                    vt3 = vtm[j8][:].rearrange("p (h c) -> p h c", h=GH)
                    scalar.copy(out=vt3[:, :, 0:64],
                                in_=v_all.rearrange("p (h c) -> p h c",
                                                    h=GH)).then_inc(s_vv, 1)
                if w < NTILE:
                    i = w
                    scalar.wait_ge(s_x1[i % 3], 16 * (i // 3 + 1))
                    if i >= 1:
                        scalar.wait_ge(s_sa1, i)
                    scalar.activation(out=asq[:], in_=xb1[i % 3][:],
                                      func=AF.Copy, scale=1.0,
                                      accum_out=sum_a[:, i:i + 1]
                                      ).then_inc(s_sa1, 1)
                if w < 8:
                    c = w
                    scalar.wait_ge(s_wlk[c % 2], 16 * (c // 2 + 1))
                    scalar.copy(out=wkb[c][:], in_=wk_f[c % 2][:]).then_inc(s_wc, 1)
                    scalar.wait_ge(s_wlv[c % 2], 16 * (c // 2 + 1))
                    scalar.copy(out=wvb[c][:], in_=wv_f[c % 2][:]).then_inc(s_wc, 1)
                i4 = w - 4
                if 0 <= i4 < NTILE:
                    g4 = i4 // 4
                    scalar.wait_ge(s_x2[i4 % 2], 16 * (i4 // 2 + 1))
                    scalar.wait_ge(s_smg, g4 + 1)
                    if i4 >= 2:
                        scalar.wait_ge(s_xqi, i4 - 1)
                    scalar.activation(out=xq_f[i4 % 2][:], in_=xb2[i4 % 2][:],
                                      func=AF.Identity,
                                      scale=sca2_a[:, i4:i4 + 1],
                                      bias=nb2c_a[:, i4:i4 + 1]
                                      ).then_inc(s_xqf, 1)
                    if i4 >= 2:
                        scalar.wait_ge(s_t12, 2 * (i4 - 2) + 2)
                    scalar.activation(out=xn_b[i4 % 2][:], in_=xb2[i4 % 2][:],
                                      func=AF.Identity,
                                      scale=rstd_a[:, i4:i4 + 1],
                                      bias=negmr_a[:, i4:i4 + 1]
                                      ).then_inc(s_xnb, 1)
                if w in mathwave:
                    g = mathwave.index(w)
                    c0, c1 = 4 * g, min(4 * g + 4, NTILE)
                    scalar.wait_ge(s_vvg, g + 1)
                    scalar.activation(out=sqa_a[:, c0:c1], in_=var_a[:, c0:c1],
                                      func=AF.Sqrt, bias=ct[:, 0:1],
                                      scale=1.0).then_inc(s_sqg, 1)
                if 0 <= i6 < NTILE:
                    for ci, key in enumerate([("t34", i6), ("t34x", i6)]):
                        scalar.wait_ge(s_t34, 2 * i6 + 1 + ci)
                        if ci == 0 and i6 >= 2:
                            scalar.wait_ge(s_soxq[i6 % 2], so_cnt(i6 - 2))
                        scalar.copy(out=xq_st[i6 % 2][:, 512 * ci:512 * (ci + 1)],
                                    in_=pool[ub[key]][:]).then_inc(s_ev, 1)
                i5 = w - 5
                if 0 <= i5 < NTILE:
                    for ci, key in enumerate([("t12", i5), ("t12x", i5)]):
                        scalar.wait_ge(s_t12, 2 * i5 + 1 + ci)
                        if ci == 0 and i5 >= 2:
                            scalar.wait_ge(s_soxn[i5 % 2], so_cnt(i5 - 2))
                            scalar.wait_ge(s_mm, i5 - 2)
                        scalar.copy(out=xn_st[i5 % 2][:, 512 * ci:512 * (ci + 1)],
                                    in_=pool[ub[key]][:]).then_inc(s_ev, 1)
                if 0 <= j8 < NOWN:
                    scalar.wait_ge(s_et, j8 + 1)
                    if j8 >= 1:
                        scalar.wait_ge(s_kv, j8)
                    scalar.activation(out=eexp[j8 % 2][:], in_=etmp[:],
                                      func=AF.Exp, bias=ct[:, 2:3],
                                      scale=1.0).then_inc(s_ee, 1)

        @blk.tensor
        def _(tensor):
            tensor.wait_ge(s_c, 32)
            tensor.wait_ge(s_ini, 1)

            def bank_wait(key):
                bw = uw[key]
                if bw is not None:
                    tensor.wait_ge(s_ev if bw[0] == "ev" else s_su, bw[1])

            for w in range(NWAVE):
                i6 = w - 6
                if 0 <= i6 < NTILE:
                    tensor.wait_ge(s_qb, i6 + 1)
                    for ci, key in enumerate([("t34", i6), ("t34x", i6)]):
                        bank_wait(key)
                        bank = pool[ub[key]]
                        for q in range(4):
                            cc = 4 * ci + q
                            mm = tensor.matmul(
                                out=bank[:, 128 * q:128 * (q + 1)],
                                lhsT=xq_b[i6 % 2][:, 128 * cc:128 * (cc + 1)],
                                rhs=idt[:], start=True, stop=True)
                        mm.then_inc(s_t34, 1)
                i5 = w - 5
                if 0 <= i5 < NTILE:
                    tensor.wait_ge(s_xnb, i5 + 1)
                    for ci, key in enumerate([("t12", i5), ("t12x", i5)]):
                        bank_wait(key)
                        bank = pool[ub[key]]
                        for q in range(4):
                            cc = 4 * ci + q
                            mm = tensor.matmul(
                                out=bank[:, 128 * q:128 * (q + 1)],
                                lhsT=xn_b[i5 % 2][:, 128 * cc:128 * (cc + 1)],
                                rhs=idt[:], start=True, stop=True)
                        mm.then_inc(s_t12, 1)
                j = w - 7
                if 0 <= j < NOWN:
                    tensor.wait_ge(s_ev, evcnt[("t12x", j + 1)])
                    if j >= 1:
                        tensor.wait_ge(s_et, j)
                        tensor.wait_ge(s_km, j)
                        tensor.wait_ge(s_vv, j)
                    stg = xn_st[(j + 1) % 2]
                    for c in range(8):
                        if j <= 1:
                            tensor.wait_ge(s_wc, 2 * c + 2)
                        st = (c == 0)
                        sp = (c == 7)
                        lhsT = stg[:, 128 * c:128 * (c + 1)]
                        tensor.matmul(out=kvt[:, 0:512], lhsT=lhsT,
                                      rhs=wkb[c][:, 0:512], start=st, stop=sp)
                        tensor.matmul(out=kvt[:, 512:768], lhsT=lhsT,
                                      rhs=wkb[c][:, 512:768], start=st, stop=sp)
                        tensor.matmul(out=kvt[:, 1024:1536], lhsT=lhsT,
                                      rhs=wvb[c][:, 0:512], start=st, stop=sp)
                        mm = tensor.matmul(out=kvt[:, 1536:1792], lhsT=lhsT,
                                           rhs=wvb[c][:, 512:768], start=st,
                                           stop=sp)
                    mm.then_inc(s_mm, 1)
                j2 = w - 10
                if 0 <= j2 < NOWN:
                    tensor.wait_ge(s_kv, j2 + 1)
                    tensor.wait_ge(s_vv, j2 + 1)
                    for s in range(2):
                        u = 2 * j2 + s
                        bank_wait(("cx", u))
                        bank = pool[ub[("cx", u)]]
                        for h in range(GH):
                            pr = h % 2
                            pj = h // 2
                            mm = tensor.matmul(
                                out=bank[64 * pr:64 * pr + 64,
                                         65 * pj:65 * pj + 65],
                                lhsT=ktm[j2][64 * s:64 * s + 64,
                                             64 * h:64 * h + 64],
                                rhs=vtm[j2][64 * s:64 * s + 64,
                                            65 * h:65 * h + 65],
                                start=True, stop=True,
                                tile_position=(64 * s, 64 * pr))
                        mm.then_inc(s_cx, 1)

        @blk.gpsimd
        def _(gpsimd):
            gpsimd.wait_ge(s_ini, 1)
            gpsimd.dma_start(out=states[0], in_=st_ring[:, 0:SW]).then_inc(s_sd[0], 16)
            for w in range(NWAVE):
                i6 = w - 6
                if 0 <= i6 < NTILE:
                    gpsimd.wait_ge(s_ev, evcnt[("t12x", i6)])
                    gpsimd.dma_start(
                        out=xnfm[:, 128 * i6:128 * (i6 + 1)].rearrange(
                            "(c p) t -> p c t", p=128),
                        in_=xn_st[i6 % 2][:].rearrange("p (c t) -> p c t", c=8)
                        ).then_inc(s_soxn[i6 % 2], 16)
                i7 = w - 7
                if 0 <= i7 < NTILE:
                    gpsimd.wait_ge(s_ev, evcnt[("t34x", i7)])
                    gpsimd.dma_start(
                        out=xqfm[:, 128 * i7:128 * (i7 + 1)].rearrange(
                            "(c p) t -> p c t", p=128),
                        in_=xq_st[i7 % 2][:].rearrange("p (c t) -> p c t", c=8)
                        ).then_inc(s_soxq[i7 % 2], 16)
                j3s = w - 12
                if 0 <= j3s < NOWN:
                    for s in range(2):
                        u = 2 * j3s + s
                        gpsimd.wait_ge(s_su, u + 1)
                        gpsimd.dma_start(
                            out=states[u + 1],
                            in_=st_ring[:, SW * ((u + 1) % 3):
                                        SW * ((u + 1) % 3) + SW]
                            ).then_inc(s_sd[(u + 1) % 3], 16)
            gpsimd.wait_ge(s_smg, NGRP)
            gpsimd.dma_start(out=gam, in_=gam_a[:, 1:NTILE]).then_inc(s_fo, 16)
            gpsimd.wait_ge(s_fo, 16)
            for di, arr in enumerate([mean_a, var_a, xmax_a, xmin_a, ssq_a, asc_a,
                                      gam_a, rstd_a]):
                gpsimd.dma_start(out=dbg[:, NTILE * di:NTILE * (di + 1)],
                                 in_=arr[:]).then_inc(s_fo, 16)
            gpsimd.wait_ge(s_fo, 16 + 8 * 16)
            for r_ in range(3):
                gpsimd.wait_ge(s_sd[r_], 16 * 11)
            gpsimd.wait_ge(s_soxn[0], 16 * 9)
            gpsimd.wait_ge(s_soxn[1], 16 * 8)
            gpsimd.wait_ge(s_soxq[0], 16 * 9)
            gpsimd.wait_ge(s_soxq[1], 16 * 8)

    return nc




LH = 4
LD = LH * DH
NG = 5
GSZ = [512, 512, 512, 512, 128]


class _Ring:

    def __init__(self, tensor, pall, sems):
        self.t = tensor
        self.pall = pall
        self.sems = sems
        self.ptr = 0
        self.last = [None] * 8
        self.uses = 0

    def claim(self, n, cons_list):
        assert len(cons_list) == n
        if self.ptr + n > 8:
            self.ptr = 0
        b0 = self.ptr
        self.ptr += n
        if self.ptr >= 8:
            self.ptr = 0
        for k, b in enumerate(range(b0, b0 + n)):
            w = self.last[b]
            if w is not None:
                self.t.wait_ge(self.sems[w[0]], w[1])
            self.last[b] = cons_list[k]
        idx = self.uses
        self.uses += 1
        return self.pall[:, 512 * b0:512 * (b0 + n)], idx


def build_l2(detect=False):
    nc = bass.Bass("TRN2", target_bir_lowering=False, debug=False,
                   detect_race_conditions=detect, dynamic_dma_scratch_size=8192)

    xown = nc.dram_tensor("xown", [T, D], F32, kind="ExternalInput").ap()
    xnfm_i = nc.dram_tensor("xnfm", [D, TT], BF16, kind="ExternalInput").ap()
    xqfm_i = nc.dram_tensor("xqfm", [D, TT], BF16, kind="ExternalInput").ap()
    gam_i = nc.dram_tensor("gam", [128, NOWN], F32, kind="ExternalInput").ap()
    sloc_i = nc.dram_tensor("sloc", [NCHUNK + 1, 128, SW], F32,
                            kind="ExternalInput").ap()
    tails_i = nc.dram_tensor("tails", [NCORE, 128, SW], F32,
                             kind="ExternalInput").ap()
    pm_i = nc.dram_tensor("pmask", [128, NCORE], F32, kind="ExternalInput").ap()
    wq_i = nc.dram_tensor("wq", [D, D], F32, kind="ExternalInput").ap()
    wo_i = nc.dram_tensor("wo", [D, D], F32, kind="ExternalInput").ap()
    wkl_i = nc.dram_tensor("wkl", [D, LD], F32, kind="ExternalInput").ap()
    wvl_i = nc.dram_tensor("wvl", [D, LD], F32, kind="ExternalInput").ap()
    bo_i = nc.dram_tensor("bo", [1, D], F32, kind="ExternalInput").ap()
    bias_i = nc.dram_tensor("bias", [1, D], F32, kind="ExternalInput").ap()
    packed_i = nc.dram_tensor("packed", [256, D], I32, kind="ExternalInput").ap()
    csts = nc.dram_tensor("csts", [128, 6], F32, kind="ExternalInput").ap()
    mloc_i = nc.dram_tensor("mloc", [128, 1024], BF16, kind="ExternalInput").ap()
    mhal_i = nc.dram_tensor("mhal", [128, 1024], BF16, kind="ExternalInput").ap()
    idn = nc.dram_tensor("idn", [128, 128], BF16, kind="ExternalInput").ap()

    out_o = nc.dram_tensor("out", [T, D], F32, kind="ExternalOutput").ap()

    ctx = contextlib.ExitStack()
    with ctx:
        sb = lambda name, shape, dt: ctx.enter_context(nc.sbuf_tensor(name, shape, dt))
        sem = lambda name: ctx.enter_context(nc.semaphore(name))

        TOP = 207872
        szA, szB, szC, szD, szE, szF = 34816, 17408, 12480, 8192, 12288, 16384
        ovlA = TOP - szA
        ovlB = ovlA - szB
        ovlC = ovlB - szC
        ovlD = ovlC - szD
        ovlE = ovlD - szE
        ovlF = ovlE - szF
        xnr_all = nc.alloc_sbuf_tensor_at("xnr_all", [128, 8 * TT], BF16,
                                          offset=ovlA)
        xnr = [xnr_all.ap()[:, TT * c:TT * (c + 1)] for c in range(8)]
        afm_t = nc.alloc_sbuf_tensor_at("afm_t", [128, 8 * T], BF16, offset=ovlA)
        afm = afm_t.ap()

        dec_all = nc.alloc_sbuf_tensor_at("dec_all", [128, 3 * D], I32,
                                          offset=ovlB)
        pkd = [dec_all.ap()[:, D * c:D * (c + 1)] for c in range(2)]
        crumb = dec_all.ap()[:, 2 * D:3 * D]
        wrow_all = nc.alloc_sbuf_tensor_at("wrow_all", [128, 2 * D], BF16,
                                           offset=ovlB + 12 * 1024)
        wrow = [wrow_all.ap()[:, D * i:D * (i + 1)] for i in range(2)]
        lfl_all = nc.alloc_sbuf_tensor_at("lfl_all", [128, 4 * TT], BF16,
                                          offset=ovlB)
        qfl = [lfl_all.ap()[:, TT * i:TT * (i + 1)] for i in range(2)]
        kfl = [lfl_all.ap()[:, TT * (2 + i):TT * (3 + i)] for i in range(2)]

        tails_t = nc.alloc_sbuf_tensor_at("tails_t", [128, NCORE * SW], F32,
                                          offset=ovlC)
        tails_sb = tails_t.ap()
        xqs_all = nc.alloc_sbuf_tensor_at("xqs_all", [128, 2 * D], BF16,
                                          offset=ovlC)
        xqs = [xqs_all.ap()[:, D * i:D * (i + 1)] for i in range(2)]
        tmpf_t = nc.alloc_sbuf_tensor_at("tmpf_t", [128, D], F32,
                                         offset=ovlC + 4 * 1024)
        tmpf = tmpf_t.ap()

        wstg_t = nc.alloc_sbuf_tensor_at("wstg_t", [128, 2 * D], F32,
                                         offset=ovlD)
        wstg = [wstg_t.ap()[:, D * i:D * (i + 1)] for i in range(2)]
        xrs_all = nc.alloc_sbuf_tensor_at("xrs_all", [128, 2 * D], F32,
                                          offset=ovlD)
        xrs = [xrs_all.ap()[:, D * i:D * (i + 1)] for i in range(2)]

        qe_all = nc.alloc_sbuf_tensor_at("qe_all", [128, 2 * GD], F32,
                                         offset=ovlE)
        qe_sb = [qe_all.ap()[:, GD * i:GD * (i + 1)] for i in range(2)]
        qtm_t = nc.alloc_sbuf_tensor_at("qtm_t", [128, 2 * GD], BF16,
                                        offset=ovlE + 6 * 1024)
        qtm = [qtm_t.ap()[:, GD * i:GD * (i + 1)] for i in range(2)]
        aw_all = nc.alloc_sbuf_tensor_at("aw_all", [128, 6 * 1024], BF16,
                                         offset=ovlE)
        attw = [aw_all.ap()[:, 1024 * i:1024 * (i + 1)] for i in range(2)]
        attn_ = [aw_all.ap()[:, 1024 * (2 + i):1024 * (3 + i)] for i in range(2)]
        attT = [aw_all.ap()[:, 1024 * (4 + i):1024 * (5 + i)] for i in range(2)]

        wqb_all = nc.alloc_sbuf_tensor_at("wqb_all", [128, 8 * D], BF16,
                                          offset=ovlF)
        wqb = [wqb_all.ap()[:, D * c:D * (c + 1)] for c in range(8)]
        wob_all = nc.alloc_sbuf_tensor_at("wob_all", [128, 8 * D], BF16,
                                          offset=ovlF)
        wob = [wob_all.ap()[:, D * c:D * (c + 1)] for c in range(8)]

        wklb = [sb(f"wklb{c}", [128, LD], BF16) for c in range(8)]
        wvlb = [sb(f"wvlb{c}", [128, LD], BF16) for c in range(8)]
        wstg2 = sb("wstg2", [128, 2 * LD], F32)
        wffn = sb("wffn", [128, 8 * D], BF16)
        ct = sb("ct", [128, 6], F32)
        idt = sb("idt", [128, 128], BF16)
        mlo = sb("mlo", [128, 1024], BF16)
        mha = sb("mha", [128, 1024], BF16)
        ones_b = sb("ones_b", [1, 128], BF16)
        borow_f = sb("borow_f", [1, D], F32)
        borow = sb("borow", [1, D], BF16)
        birow_f = sb("birow_f", [1, D], F32)
        gam_sb = sb("gam_sb", [128, NOWN], F32)
        pm_sb = sb("pm_sb", [128, NCORE], F32)
        qs12 = sb("qs12", [128, GH], F32)
        qr12 = sb("qr12", [128, GH], F32)
        qfm = sb("qfm", [128, 6 * T], BF16)
        vtl = [sb(f"vtl{t}", [128, LD], BF16) for t in range(NTILE)]
        dsum = sb("dsum", [128, 4], F32)
        drc = sb("drc", [128, 4], F32)
        offs = sb("offs", [128, SW], F32)
        sg = [sb(f"sg{i}", [128, SW], F32) for i in range(3)]
        sgb = sb("sgb", [128, 2 * SW], F32)
        zt6 = sb("zt6", [128, 12], F32)
        rz6 = sb("rz6", [128, 12], F32)
        ctxn_b = sb("ctxn_b", [128, 768], BF16)
        outf = [sb(f"outf{i}", [128, D], F32) for i in range(2)]

        pall = ctx.enter_context(nc.psum_tensor("pall", [128, 4096], F32))

        s_ct = sem("s_ct")
        s_idt = sem("s_idt")
        s_mlo = sem("s_mlo")
        s_mha = sem("s_mha")
        s_pm = sem("s_pm")
        s_gam = sem("s_gam")
        s_tails = sem("s_tails")
        s_bo = sem("s_bo")
        s_bi = sem("s_bi")
        s_xn = sem("s_xn")
        s_xn2 = sem("s_xn2")
        s_wl = [sem(f"s_wl{i}") for i in range(2)]
        s_wl2 = sem("s_wl2")
        s_wcq = sem("s_wcq")
        s_wco = sem("s_wco")
        s_wclv = sem("s_wclv")
        s_rows = sem("s_rows")
        s_bsum = sem("s_bsum")
        s_ev2 = sem("s_ev2")
        s_pk = sem("s_pk")
        s_dc = sem("s_dc")
        s_pe = sem("s_pe")
        s_ev = sem("s_ev")
        s_qe = sem("s_qe")
        s_qt = sem("s_qt")
        s_msk = sem("s_msk")
        s_lxe = sem("s_lxe")
        s_att = sem("s_att")
        s_ctxn = sem("s_ctxn")
        s_sgl = [sem(f"s_sgl{i}") for i in range(3)]
        s_sgc = sem("s_sgc")
        s_off = sem("s_off")
        s_xql = [sem(f"s_xql{i}") for i in range(2)]
        s_xrl = [sem(f"s_xrl{i}") for i in range(2)]
        s_f1 = sem("s_f1")
        s_f2 = sem("s_f2")
        s_so = [sem(f"s_so{i}") for i in range(2)]
        s_vr = sem("s_vr")
        s_evv = sem("s_evv")

        sems = {"ev": s_ev, "ev2": s_ev2, "evv": s_evv, "qe": s_qe,
                "lxe": s_lxe, "f1": s_f1, "f2": s_f2}

        blk = ctx.enter_context(nc.Block())

        C = {"dec": [], "p1q": [], "p1t": [], "p2v": [], "p2g": [], "p3d": [],
             "p3t": [], "p3a": [], "p4": [], "p5f": [], "p5o": []}
        evn = [0]

        def evc():
            evn[0] += 1
            return evn[0]

        @blk.tensor
        def _(tensor):
            ring = _Ring(tensor, pall, sems)
            tensor.wait_ge(s_ct, 16)
            tensor.wait_ge(s_idt, 16)

            for pr in range(8):
                tensor.wait_ge(s_dc, pr + 1)
                for half in range(2):
                    e = evc()
                    bank, idx = ring.claim(1, [("ev", e)])
                    C["dec"].append((bank, idx, e))
                    for q in range(4):
                        ic = 4 * half + q
                        mm = tensor.matmul(
                            out=bank[:, 128 * q:128 * (q + 1)],
                            lhsT=wrow[pr % 2][:, 128 * ic:128 * (ic + 1)],
                            rhs=idt[:], start=True, stop=True)
                    mm.then_inc(s_pe, 1)

            tensor.wait_ge(s_xn, 16 * 8)
            tensor.wait_ge(s_wcq, 8)
            xn2_waited = [False]

            def p1_tr(t2):
                tensor.wait_ge(s_qt, t2 + 1)
                for half in range(2):
                    nt = 4 if half == 0 else 2
                    e = evc()
                    bank, idx = ring.claim(1, [("ev", e)])
                    C["p1t"].append((bank, idx, e))
                    for q in range(nt):
                        r = 4 * half + q
                        mm = tensor.matmul(
                            out=bank[:, 128 * q:128 * (q + 1)],
                            lhsT=qtm[t2 % 2][:, 128 * r:128 * (r + 1)],
                            rhs=idt[:], start=True, stop=True)
                    mm.then_inc(s_pe, 1)

            for t in range(NOWN):
                if t >= 7 and not xn2_waited[0]:
                    tensor.wait_ge(s_xn2, 16 * 8)
                    xn2_waited[0] = True
                bank, idx = ring.claim(2, [("qe", t + 1), ("qe", t + 1)])
                C["p1q"].append((bank, idx, None))
                for c in range(8):
                    st, sp = c == 0, c == 7
                    lhsT = xnr[c][:, HALO + 128 * t:HALO + 128 * (t + 1)]
                    tensor.matmul(out=bank[:, 0:512], lhsT=lhsT,
                                  rhs=wqb[c][:, 256:768], start=st, stop=sp)
                    mm = tensor.matmul(out=bank[:, 512:768], lhsT=lhsT,
                                       rhs=wqb[c][:, 768:1024], start=st, stop=sp)
                mm.then_inc(s_pe, 1)
                if t >= 1:
                    p1_tr(t - 1)
            p1_tr(NOWN - 1)

            tensor.wait_ge(s_xn2, 16 * 8)
            tensor.wait_ge(s_wclv, 16)
            for t in range(NTILE):
                e = evc()
                bank, idx = ring.claim(1, [("ev", e)])
                C["p2v"].append((bank, idx, e))
                for c in range(8):
                    mm = tensor.matmul(out=bank[:, 0:256],
                                       lhsT=xnr[c][:, 128 * t:128 * (t + 1)],
                                       rhs=wvlb[c][:], start=c == 0, stop=c == 7)
                mm.then_inc(s_pe, 1)
            evv_n = [0]
            for g in range(NG):
                n = GSZ[g]
                for part in range(4):
                    o2 = part % 2
                    if part < 2:
                        evv_n[0] += 1
                        bank, idx = ring.claim(1, [("evv", evv_n[0])])
                        C["p2g"].append((bank, idx, None))
                        for c in range(8):
                            mm = tensor.matmul(
                                out=bank[:, 0:n],
                                lhsT=wqb[c][:, 128 * o2:128 * (o2 + 1)],
                                rhs=xnr[c][:, 512 * g:512 * g + n],
                                start=c == 0, stop=c == 7)
                        mm.then_inc(s_pe, 1)
                        continue
                    e = evc()
                    bank, idx = ring.claim(1, [("ev", e)])
                    C["p2g"].append((bank, idx, e))
                    for c in range(8):
                        if part < 2:
                            rhsw = wqb[c][:, 128 * o2:128 * (o2 + 1)]
                        else:
                            rhsw = wklb[c][:, 128 * o2:128 * (o2 + 1)]
                        mm = tensor.matmul(
                            out=bank[:, 0:n],
                            lhsT=rhsw,
                            rhs=xnr[c][:, 512 * g:512 * g + n],
                            start=c == 0, stop=c == 7)
                    mm.then_inc(s_pe, 1)

            def p3_tr(b2):
                tensor.wait_ge(s_att, b2 + 1)
                for half in range(2):
                    e = evc()
                    bank, idx = ring.claim(1, [("ev", e)])
                    C["p3t"].append((bank, idx, e))
                    for q in range(4):
                        r = 4 * half + q
                        mm = tensor.matmul(
                            out=bank[:, 128 * q:128 * (q + 1)],
                            lhsT=attn_[b2 % 2][:, 128 * r:128 * (r + 1)],
                            rhs=idt[:], start=True, stop=True)
                    mm.then_inc(s_pe, 1)

            def p3_av(b2):
                tensor.wait_ge(s_ev, C["p3t"][2 * b2 + 1][2])
                e = evc()
                bank, idx = ring.claim(1, [("ev", e)])
                C["p3a"].append((bank, idx, e))
                for h in range(4):
                    for cc in range(2):
                        sl = 256 * (2 * (h % 2) + h // 2) + 128 * cc
                        mm = tensor.matmul(
                            out=bank[64 * (h % 2):64 * (h % 2) + 64,
                                     128 * (h // 2):128 * (h // 2) + 128],
                            lhsT=vtl[b2 + cc][:, 64 * h:64 * (h + 1)],
                            rhs=attT[b2 % 2][:, sl:sl + 128],
                            start=cc == 0, stop=cc == 1,
                            tile_position=(0, 64 * (h % 2)))
                mm.then_inc(s_pe, 1)

            for b in range(NOWN):
                bank, idx = ring.claim(2, [("lxe", b + 1), ("lxe", b + 1)])
                C["p3d"].append((bank, idx, None))
                for h in (0, 2, 1, 3):
                    colh = 512 * (h % 2) + 256 * (h // 2)
                    mm = tensor.matmul(
                        out=bank[:, colh:colh + 256],
                        lhsT=qfl[h // 2][64 * (h % 2):64 * (h % 2) + 64,
                                         HALO + 128 * b:HALO + 128 * (b + 1)],
                        rhs=kfl[h // 2][64 * (h % 2):64 * (h % 2) + 64,
                                        128 * b:128 * b + 256],
                        start=True, stop=True,
                        tile_position=(64 * (h % 2), 0))
                mm.then_inc(s_pe, 1)
                if b >= 1:
                    p3_tr(b - 1)
                if b >= 2:
                    p3_av(b - 2)
            p3_tr(NOWN - 1)
            p3_av(NOWN - 2)
            p3_av(NOWN - 1)

            tensor.wait_ge(s_ev, C["p1t"][-1][2])
            for u in range(NCHUNK):
                tensor.wait_ge(s_ctxn, 2 * (u // 2) + 2)
                e1 = evc()
                bank, idx = ring.claim(2, [("ev", e1), ("ev2", u + 1)])
                C["p4"].append((bank, idx, e1))
                for h in range(GH):
                    j = h // 2
                    pr = h % 2
                    mm = tensor.matmul(
                        out=bank[64 * pr:64 * pr + 64,
                                 512 * pr + 64 * j:512 * pr + 64 * (j + 1)],
                        lhsT=ctxn_b[64 * pr:64 * pr + 64,
                                    384 * (u % 2) + 64 * j:
                                    384 * (u % 2) + 64 * (j + 1)],
                        rhs=qfm[64 * pr:64 * pr + 64,
                                T * j + 64 * u:T * j + 64 * (u + 1)],
                        start=True, stop=True,
                        tile_position=(64 * pr, 64 * pr))
                mm.then_inc(s_pe, 1)

            tensor.wait_ge(s_ev, evn[0])
            tensor.wait_ge(s_ev2, NCHUNK)
            tensor.wait_ge(s_wco, 8)
            tensor.wait_ge(s_rows, 1)
            for t in range(NOWN):
                tensor.wait_ge(s_xql[t % 2], 16 * (t // 2 + 1))
                bankF, idxF = ring.claim(2, [("f2", t + 1), ("f2", t + 1)])
                C["p5f"].append((bankF, idxF, None))
                for c in range(8):
                    st, sp = c == 0, c == 7
                    lhsT = xqs[t % 2][:, 128 * c:128 * (c + 1)]
                    tensor.matmul(out=bankF[:, 0:512], lhsT=lhsT,
                                  rhs=wffn[:, 1024 * c:1024 * c + 512],
                                  start=st, stop=sp)
                    mm = tensor.matmul(out=bankF[:, 512:1024], lhsT=lhsT,
                                       rhs=wffn[:, 1024 * c + 512:1024 * (c + 1)],
                                       start=st, stop=sp)
                mm.then_inc(s_pe, 1)
                bankO, idxO = ring.claim(2, [("f1", t + 1), ("f1", t + 1)])
                C["p5o"].append((bankO, idxO, None))
                for half in range(2):
                    for c in range(8):
                        tensor.matmul(
                            out=bankO[:, 512 * half:512 * (half + 1)],
                            lhsT=afm[:, 2048 * c + 128 * t:2048 * c + 128 * (t + 1)],
                            rhs=wob[c][:, 512 * half:512 * (half + 1)],
                            start=c == 0, stop=False)
                    mm = tensor.matmul(
                        out=bankO[:, 512 * half:512 * (half + 1)],
                        lhsT=ones_b[:],
                        rhs=borow[0:1, 512 * half:512 * (half + 1)],
                        start=False, stop=True)
                mm.then_inc(s_pe, 1)

        @blk.vector
        def _(vector):
            vrn = [0]

            def vrt(instr):
                vrn[0] += 1
                instr.then_inc(s_vr, 1)
                vector.wait_ge(s_vr, vrn[0])

            vector.memset(ones_b[:], 1.0)
            vector.wait_ge(s_bo, 16)
            vector.wait_ge(s_bi, 16)
            vector.tensor_tensor(out=borow_f[:], in0=borow_f[:],
                                 in1=birow_f[:], op=ALU.add).then_inc(s_bsum, 1)
            vector.wait_ge(s_pk, 32)
            for pr in range(8):
                s, c = divmod(pr, 2)
                if pr >= 2:
                    vector.wait_ge(s_pe, C["dec"][2 * (pr - 2) + 1][1] + 1)
                vrt(vector.tensor_scalar(out=crumb, in0=pkd[c], scalar1=2 * s,
                                         scalar2=3, op0=ALU.logical_shift_right,
                                         op1=ALU.bitwise_and))
                vector.tensor_scalar(out=wrow[pr % 2], in0=crumb,
                                     scalar1=-1.0, scalar2=None,
                                     op0=ALU.add).then_inc(s_dc, 1)
                vector.wait_ge(s_dc, pr + 1)

            vector.wait_ge(s_tails, 16)
            vector.wait_ge(s_pm, 16)
            vrt(vector.tensor_scalar(out=offs[:], in0=tails_sb[:, 0:SW],
                                     scalar1=pm_sb[:, 0:1], scalar2=None,
                                     op0=ALU.mult))
            for j in range(1, NCORE):
                ins = vector.scalar_tensor_tensor(
                    out=offs[:], in0=tails_sb[:, SW * j:SW * (j + 1)],
                    scalar=pm_sb[:, j:j + 1], in1=offs[:], op0=ALU.mult,
                    op1=ALU.add)
                if j == NCORE - 1:
                    ins.then_inc(s_off, 1)
                    vector.wait_ge(s_off, 1)
                else:
                    vrt(ins)

            for t in range(NOWN):
                vector.wait_ge(s_qe, t + 1)
                qe = qe_sb[t % 2]
                vrt(vector.reduce_sum(out=qs12[:],
                                      in_=qe[:].rearrange("p (h d) -> p h d",
                                                          h=GH), axis=AX))
                vrt(vector.reciprocal(out=qr12[:], in_=qs12[:]))
                if t >= 2:
                    vector.wait_ge(s_pe, C["p1t"][2 * (t - 2) + 1][1] + 1)
                vector.tensor_tensor(
                    out=qtm[t % 2][:].rearrange("p (h d) -> p h d", h=GH),
                    in0=qe[:].rearrange("p (h d) -> p h d", h=GH),
                    in1=qr12[:].unsqueeze(2).broadcast_to([128, GH, DH]),
                    op=ALU.mult).then_inc(s_qt, 1)

            gi2 = 0
            for g in range(NG):
                n = GSZ[g]
                for part in range(4):
                    bank, idx, _e = C["p2g"][gi2]
                    gi2 += 1
                    if part >= 2:
                        continue
                    vector.wait_ge(s_pe, idx + 1)
                    vector.tensor_scalar(out=qfl[part % 2][:, 512 * g:512 * g + n],
                                         in0=bank[:, 0:n], scalar1=0.125,
                                         scalar2=None,
                                         op0=ALU.mult).then_inc(s_evv, 1)

            vector.wait_ge(s_mlo, 16)
            vector.wait_ge(s_mha, 16)
            for b in range(NOWN):
                dbank = C["p3d"][b][0]
                vector.wait_ge(s_pe, C["p3d"][b][1] + 1)
                if b == 0:
                    vrt(vector.tensor_tensor(out=dbank[:, 0:1024],
                                             in0=dbank[:, 0:1024],
                                             in1=mha[:], op=ALU.add))
                vector.tensor_tensor(out=dbank[:, 0:1024], in0=dbank[:, 0:1024],
                                     in1=mlo[:], op=ALU.add).then_inc(s_msk, 1)
                vector.wait_ge(s_lxe, b + 1)
                vrt(vector.reciprocal(out=drc[:], in_=dsum[:]))
                if b >= 2:
                    vector.wait_ge(s_pe, C["p3t"][2 * (b - 2) + 1][1] + 1)
                vector.tensor_tensor(
                    out=attn_[b % 2][:].rearrange("p (e f) -> p e f", e=4),
                    in0=attw[b % 2][:].rearrange("p (e f) -> p e f", e=4),
                    in1=drc[:].unsqueeze(2).broadcast_to([128, 4, 256]),
                    op=ALU.mult).then_inc(s_att, 1)

            a3v = afm[:].rearrange("p (g t) -> p g t", g=8)
            sgb4 = sgb[:].rearrange("p (c j w) -> p c j w", c=2, j=6)
            for v in range(NCHUNK // 2):
                u0, u1 = 2 * v, 2 * v + 1
                if v >= 1:
                    vector.wait_ge(s_ctxn, 2 * v)
                vector.wait_ge(s_sgl[u0 % 3], 16 * (u0 // 3 + 1))
                vector.tensor_tensor(out=sgb[:, 0:SW], in0=sg[u0 % 3][:],
                                     in1=offs[:], op=ALU.add)
                vector.wait_ge(s_sgl[u1 % 3], 16 * (u1 // 3 + 1))
                vector.tensor_tensor(out=sgb[:, SW:2 * SW], in0=sg[u1 % 3][:],
                                     in1=offs[:], op=ALU.add).then_inc(s_sgc, 2)
                vector.wait_ge(s_sgc, u1 + 1)
                vrt(vector.tensor_scalar(out=zt6[:], in0=sgb4[:, :, :, 64],
                                         scalar1=1e-6, scalar2=None,
                                         op0=ALU.add))
                vrt(vector.reciprocal(out=rz6[:], in_=zt6[:]))
                if v >= 1:
                    vector.wait_ge(s_pe, C["p4"][u1 - 2][1] + 1)
                vector.tensor_tensor(
                    out=ctxn_b[:].rearrange("p (c j w) -> p c j w", c=2, j=6),
                    in0=sgb4[:, :, :, 0:64],
                    in1=rz6[:].rearrange("p (c j) -> p c j", c=2).unsqueeze(3
                        ).broadcast_to([128, 2, 6, 64]),
                    op=ALU.mult).then_inc(s_ctxn, 2)
                if v >= 1:
                    for uu in (2 * v - 2, 2 * v - 1):
                        ub_, ui_, _ue = C["p4"][uu]
                        vector.wait_ge(s_pe, ui_ + 1)
                        vector.tensor_copy(
                            out=a3v[64:128, 2:8, 64 * uu:64 * (uu + 1)],
                            in_=ub_[64:128, 512:896].rearrange(
                                "p (j w) -> p j w", j=6)).then_inc(s_ev2, 1)
            for uu in (NCHUNK - 2, NCHUNK - 1):
                ub_, ui_, _ue = C["p4"][uu]
                vector.wait_ge(s_pe, ui_ + 1)
                vector.tensor_copy(
                    out=a3v[64:128, 2:8, 64 * uu:64 * (uu + 1)],
                    in_=ub_[64:128, 512:896].rearrange(
                        "p (j w) -> p j w", j=6)).then_inc(s_ev2, 1)

            vector.wait_ge(s_gam, 16)
            for t in range(NOWN):
                vector.wait_ge(s_xrl[t % 2], 16 * (t // 2 + 1))
                vector.wait_ge(s_pe, C["p5o"][t][1] + 1)
                if t >= 1:
                    vector.wait_ge(s_f2, t)
                vector.scalar_tensor_tensor(
                    out=tmpf, in0=C["p5o"][t][0], scalar=1.0, in1=xrs[t % 2],
                    op0=ALU.mult, op1=ALU.add).then_inc(s_f1, 1)
                vector.wait_ge(s_f1, t + 1)
                vector.wait_ge(s_pe, C["p5f"][t][1] + 1)
                if t >= 2:
                    vector.wait_ge(s_so[t % 2], 16 * (t // 2))
                vector.scalar_tensor_tensor(
                    out=outf[t % 2][:], in0=C["p5f"][t][0],
                    scalar=gam_sb[:, t:t + 1], in1=tmpf, op0=ALU.mult,
                    op1=ALU.add).then_inc(s_f2, 1)

        @blk.scalar
        def _(scalar):
            for c in range(8):
                scalar.wait_ge(s_wl[c % 2], 16 * (c // 2 + 1))
                scalar.copy(out=wqb[c][:], in_=wstg[c % 2][:]).then_inc(s_wcq, 1)
                scalar.wait_ge(s_wcq, c + 1)
            scalar.wait_ge(s_bo, 16)
            scalar.wait_ge(s_bi, 16)
            scalar.wait_ge(s_bsum, 1)
            scalar.copy(out=borow[:], in_=borow_f[:]).then_inc(s_rows, 1)
            scalar.wait_ge(s_rows, 1)

            for pr in range(8):
                for half in range(2):
                    bank, idx, _e = C["dec"][2 * pr + half]
                    scalar.wait_ge(s_pe, idx + 1)
                    s, c = divmod(pr, 2)
                    col = 256 * s + 128 * c
                    scalar.copy(
                        out=wffn[:].rearrange("p (i o) -> p i o", i=8)[
                            :, 4 * half:4 * half + 4, col:col + 128],
                        in_=bank[:].rearrange("p (i o) -> p i o", i=4)
                    ).then_inc(s_ev, 1)

            def p1_ev(t2):
                for half in range(2):
                    bank, idx, _e = C["p1t"][2 * t2 + half]
                    scalar.wait_ge(s_pe, idx + 1)
                    nt = 4 if half == 0 else 2
                    scalar.copy(
                        out=qfm[:].rearrange("p (r t) -> p r t", r=6)[
                            :, 4 * half:4 * half + nt, 128 * t2:128 * (t2 + 1)],
                        in_=bank[:, 0:128 * nt].rearrange("p (r t) -> p r t",
                                                          r=nt)
                    ).then_inc(s_ev, 1)

            scalar.wait_ge(s_ct, 16)
            for t in range(NOWN):
                bank, idx, _e = C["p1q"][t]
                scalar.wait_ge(s_pe, idx + 1)
                if t >= 2:
                    scalar.wait_ge(s_qt, t - 1)
                scalar.activation(out=qe_sb[t % 2][:], in_=bank[:, 0:GD],
                                  func=AF.Exp, bias=ct[:, 3:4],
                                  scale=1.0).then_inc(s_qe, 1)
                if t < 8:
                    c = t
                    scalar.wait_ge(s_wl2, 32 * (c + 1))
                    scalar.copy(out=wklb[c][:],
                                in_=wstg2[:, 0:LD]).then_inc(s_wclv, 1)
                    scalar.copy(out=wvlb[c][:],
                                in_=wstg2[:, LD:2 * LD]).then_inc(s_wclv, 1)
                    scalar.wait_ge(s_wclv, 2 * (c + 1))
                if t >= 1:
                    p1_ev(t - 1)
            p1_ev(NOWN - 1)

            for t in range(NTILE):
                bank, idx, _e = C["p2v"][t]
                scalar.wait_ge(s_pe, idx + 1)
                scalar.copy(out=vtl[t][:], in_=bank[:, 0:256]).then_inc(s_ev, 1)
            gi = 0
            for g in range(NG):
                n = GSZ[g]
                for part in range(4):
                    bank, idx, _e = C["p2g"][gi]
                    gi += 1
                    if part < 2:
                        continue
                    scalar.wait_ge(s_pe, idx + 1)
                    dst = kfl[part % 2]
                    scalar.copy(out=dst[:, 512 * g:512 * g + n],
                                in_=bank[:, 0:n]).then_inc(s_ev, 1)

            def p3_ev(b2):
                if b2 >= 2:
                    scalar.wait_ge(s_pe, C["p3a"][b2 - 2][1] + 1)
                for half in range(2):
                    bank, idx, _e = C["p3t"][2 * b2 + half]
                    scalar.wait_ge(s_pe, idx + 1)
                    scalar.copy(out=attT[b2 % 2][:, 512 * half:512 * (half + 1)],
                                in_=bank[:]).then_inc(s_ev, 1)

            def p3_aev(b2):
                bank, idx, _e = C["p3a"][b2]
                scalar.wait_ge(s_pe, idx + 1)
                scalar.copy(
                    out=afm[:].rearrange("p (g t) -> p g t", g=8)[
                        :, 0:2, 128 * b2:128 * (b2 + 1)],
                    in_=bank[:, 0:256].rearrange("p (g t) -> p g t", g=2)
                ).then_inc(s_ev, 1)

            for b in range(NOWN):
                scalar.wait_ge(s_msk, b + 1)
                if b >= 1:
                    scalar.wait_ge(s_att, b)
                dbank = C["p3d"][b][0]
                for e in range(4):
                    ins = scalar.activation(
                        out=attw[b % 2][:, 256 * e:256 * (e + 1)],
                        in_=dbank[:, 256 * e:256 * (e + 1)], func=AF.Exp,
                        bias=ct[:, 2:3], scale=1.0, accum_out=dsum[:, e:e + 1])
                    if e == 3:
                        ins.then_inc(s_lxe, 1)
                if b < 8:
                    c = b
                    if c == 0:
                        scalar.wait_ge(s_pe, C["p2g"][-1][1] + 1)
                    scalar.wait_ge(s_wl[c % 2], 16 * (c // 2 + 5))
                    scalar.copy(out=wob[c][:],
                                in_=wstg[c % 2][:]).then_inc(s_wco, 1)
                    scalar.wait_ge(s_wco, c + 1)
                if b >= 1:
                    p3_ev(b - 1)
                if b >= 2:
                    p3_aev(b - 2)
            p3_ev(NOWN - 1)
            p3_aev(NOWN - 2)
            p3_aev(NOWN - 1)

            for u in range(NCHUNK):
                bank, idx, _e = C["p4"][u]
                scalar.wait_ge(s_pe, idx + 1)
                a3 = afm[:].rearrange("p (g t) -> p g t", g=8)
                scalar.copy(out=a3[0:64, 2:8, 64 * u:64 * (u + 1)],
                            in_=bank[0:64, 0:384].rearrange(
                                "p (j w) -> p j w", j=6)).then_inc(s_ev, 1)

        @blk.sync
        def _(sync):
            sync.dma_start(out=ct[:], in_=csts).then_inc(s_ct, 16)
            sync.dma_start(out=idt[:], in_=idn).then_inc(s_idt, 16)
            sync.dma_start(out=mlo[:], in_=mloc_i).then_inc(s_mlo, 16)
            sync.dma_start(out=mha[:], in_=mhal_i).then_inc(s_mha, 16)
            sync.dma_start(out=pm_sb[:], in_=pm_i).then_inc(s_pm, 16)
            sync.dma_start(out=gam_sb[:], in_=gam_i).then_inc(s_gam, 16)
            sync.dma_start(out=tails_sb[:].rearrange("p (j c) -> p j c",
                                                     j=NCORE),
                           in_=tails_i.rearrange("j p c -> p j c")
                           ).then_inc(s_tails, 16)
            sync.dma_start(out=borow_f[:], in_=bo_i).then_inc(s_bo, 16)
            sync.dma_start(out=birow_f[:], in_=bias_i).then_inc(s_bi, 16)
            for c in range(8):
                sync.dma_start(out=xnr[c][:, 0:1088],
                               in_=xnfm_i[128 * c:128 * (c + 1), 0:1088]
                               ).then_inc(s_xn, 16)
            for c in range(8):
                sync.dma_start(out=xnr[c][:, 1088:TT],
                               in_=xnfm_i[128 * c:128 * (c + 1), 1088:TT]
                               ).then_inc(s_xn2, 16)
            for u in range(NCHUNK):
                if u >= 3:
                    sync.wait_ge(s_sgc, 2 * ((u - 3) // 2) + 2)
                sync.dma_start(out=sg[u % 3][:],
                               in_=sloc_i[u]).then_inc(s_sgl[u % 3], 16)

        @blk.gpsimd
        def _(gpsimd):
            gpsimd.dma_start(out=pkd[0], in_=packed_i[0:128, :]).then_inc(s_pk, 16)
            gpsimd.dma_start(out=pkd[1],
                             in_=packed_i[128:256, :]).then_inc(s_pk, 16)
            for c in range(8):
                if c >= 2:
                    gpsimd.wait_ge(s_wcq, c - 1)
                gpsimd.dma_start(out=wstg[c % 2][:],
                                 in_=wq_i[128 * c:128 * (c + 1), :]
                                 ).then_inc(s_wl[c % 2], 16)
            for c in range(8):
                if c >= 1:
                    gpsimd.wait_ge(s_wclv, 2 * c)
                gpsimd.dma_start(out=wstg2[:, 0:LD],
                                 in_=wkl_i[128 * c:128 * (c + 1), :]
                                 ).then_inc(s_wl2, 16)
                gpsimd.dma_start(out=wstg2[:, LD:2 * LD],
                                 in_=wvl_i[128 * c:128 * (c + 1), :]
                                 ).then_inc(s_wl2, 16)
            gpsimd.wait_ge(s_wcq, 8)
            for c in range(8):
                if c >= 2:
                    gpsimd.wait_ge(s_wco, c - 1)
                gpsimd.dma_start(out=wstg[c % 2][:],
                                 in_=wo_i[128 * c:128 * (c + 1), :]
                                 ).then_inc(s_wl[c % 2], 16)
            gpsimd.wait_ge(s_off, 1)
            gpsimd.wait_ge(s_wco, 8)
            def p5_store(t2):
                gpsimd.wait_ge(s_f2, t2 + 1)
                gpsimd.dma_start(out=out_o[128 * t2:128 * (t2 + 1), :],
                                 in_=outf[t2 % 2][:]).then_inc(s_so[t2 % 2], 16)

            for t in range(NOWN):
                if t >= 2:
                    gpsimd.wait_ge(s_pe, C["p5f"][t - 2][1] + 1)
                    gpsimd.wait_ge(s_f1, t - 1)
                gpsimd.dma_start(
                    out=xqs[t % 2].rearrange("p (c t) -> p c t", c=8),
                    in_=xqfm_i[:, HALO + 128 * t:HALO + 128 * (t + 1)].rearrange(
                        "(c p) t -> p c t", p=128)).then_inc(s_xql[t % 2], 16)
                gpsimd.dma_start(out=xrs[t % 2],
                                 in_=xown[128 * t:128 * (t + 1), :]
                                 ).then_inc(s_xrl[t % 2], 16)
                if t >= 2:
                    p5_store(t - 2)
            p5_store(NOWN - 2)
            p5_store(NOWN - 1)
            gpsimd.wait_ge(s_so[0], 16 * 8)
            gpsimd.wait_ge(s_so[1], 16 * 8)

    return nc


_CACHE = {}


def _get_l1():
    if "l1" not in _CACHE:
        _CACHE["l1"] = build_l1()
    return _CACHE["l1"]


def run_l1(x_full, wk, wv, w_scale):
    import ml_dtypes
    xpad = np.concatenate([np.zeros((HALO, D), np.float32), x_full], axis=0)
    wkg = np.ascontiguousarray(wk[:, 256:])
    wvg = np.ascontiguousarray(wv[:, 256:])
    csts = np.zeros((128, 6), np.float32)
    csts[:, 0] = 1e-5
    csts[:, 1] = w_scale
    csts[:, 3] = NEGC
    csts[:, 4] = C23
    idn = np.eye(128, dtype=ml_dtypes.bfloat16)
    in_maps = []
    for c in range(NCORE):
        in_maps.append({
            "xsl": np.ascontiguousarray(xpad[T * c:T * c + TT]),
            "wkg": wkg, "wvg": wvg, "csts": csts, "idn": idn,
        })
    nc = _get_l1()
    res = run_bass_kernel_spmd(nc, in_maps, core_ids=list(range(NCORE)))
    return res.results


def _get_l2():
    if "l2" not in _CACHE:
        _CACHE["l2"] = build_l2()
    return _CACHE["l2"]


def make_l2_inmaps(inputs, res1):
    import ml_dtypes
    x = np.asarray(inputs["L0_attn_x"][0], np.float32)
    wq = np.asarray(inputs["L0_attn_Wq"], np.float32)
    wo = np.asarray(inputs["L0_attn_Wo"], np.float32)
    wk = np.asarray(inputs["L0_attn_Wk"], np.float32)
    wv = np.asarray(inputs["L0_attn_Wv"], np.float32)
    bo = np.asarray(inputs["L0_attn_bo"], np.float32).reshape(1, D)
    bias = np.asarray(inputs["L0_ffn_bias"], np.float32).reshape(1, D)
    packed = np.ascontiguousarray(
        np.asarray(inputs["L0_ffn_packed_weight"], np.int32).reshape(256, D))
    wkl = np.ascontiguousarray(wk[:, 0:LD])
    wvl = np.ascontiguousarray(wv[:, 0:LD])
    idn = np.eye(128, dtype=ml_dtypes.bfloat16)
    csts = np.zeros((128, 6), np.float32)
    csts[:, 3] = NEGC
    csts[:, 5] = 0.125
    NEGINF = np.float32(-1e30)
    m256 = np.full((128, 256), NEGC, np.float32)
    q_idx = np.arange(128)[:, None]
    k_idx = np.arange(128)[None, :]
    m256[:, 128:256] += np.where(k_idx > q_idx, NEGINF, np.float32(0.0))
    mloc = np.tile(m256, (1, 4)).astype(ml_dtypes.bfloat16)
    mhal0 = np.zeros((128, 1024), np.float32)
    for e in range(4):
        mhal0[:, 256 * e:256 * e + 128] = NEGINF
    tails = np.ascontiguousarray(
        np.stack([np.asarray(res1[c]["states"])[NCHUNK] for c in range(NCORE)], 0))
    in_maps = []
    for c in range(NCORE):
        pm = np.zeros((128, NCORE), np.float32)
        pm[:, :c] = 1.0
        mhal = (mhal0 if c == 0 else np.zeros((128, 1024), np.float32)
                ).astype(ml_dtypes.bfloat16)
        in_maps.append({
            "xown": np.ascontiguousarray(x[T * c:T * (c + 1)]),
            "xnfm": np.asarray(res1[c]["xnfm"]),
            "xqfm": np.asarray(res1[c]["xqfm"]),
            "gam": np.asarray(res1[c]["gam"]),
            "sloc": np.asarray(res1[c]["states"]),
            "tails": tails, "pmask": pm,
            "wq": wq, "wo": wo, "wkl": wkl, "wvl": wvl,
            "bo": bo, "bias": bias, "packed": packed,
            "csts": csts, "mloc": mloc, "mhal": mhal, "idn": idn,
        })
    return in_maps


def kernel(**inputs):
    x = np.asarray(inputs["L0_attn_x"][0], np.float32)
    wk = np.asarray(inputs["L0_attn_Wk"], np.float32)
    wv = np.asarray(inputs["L0_attn_Wv"], np.float32)
    scale = float(np.asarray(inputs["L0_ffn_weight_scale"]).reshape(-1)[0])
    res1 = run_l1(x, wk, wv, scale)
    in_maps = make_l2_inmaps(inputs, res1)
    nc2 = _get_l2()
    res2 = run_bass_kernel_spmd(nc2, in_maps, core_ids=list(range(NCORE)))
    out = np.concatenate([np.asarray(res2.results[c]["out"])
                          for c in range(NCORE)], axis=0)
    return out.reshape(1, 16384, D)
